# version 10 (final)
# speedup vs baseline: 1.1657x; 1.0517x over previous
import numpy as np
import ml_dtypes
import concourse.bass as bass
import concourse.mybir as mybir

F32 = mybir.dt.float32
BF16 = mybir.dt.bfloat16
AF = mybir.ActivationFunctionType
ALU = mybir.AluOpType
BF = ml_dtypes.bfloat16

L = 4096
H = 1024
NB = 64
TT = 8
RPC = 512


class Sem:
    __slots__ = ("h", "n")

    def __init__(self, h):
        self.h = h
        self.n = 0

    def bump(self, k):
        self.n += k
        return self.n


class Em:
    ENGS = ("tensor", "vector", "scalar", "gpsimd", "sync")

    def __init__(self, nc):
        self.nc = nc
        self.q = {e: [] for e in self.ENGS}
        self.auto = {}
        self.cur_scope = None
        self.last_mmode = ("__start__",)

    def set_auto(self, eng, sem):
        self.auto[eng] = sem

    def sbuf_mark(self):
        return self.nc.sbuf_base

    def sbuf_reset(self, m):
        self.nc.sbuf_base = m

    def sem(self, name):
        return Sem(self.nc.alloc_semaphore(name))

    def op(self, eng, build, waits=(), inc=None, inck=1, mmode=None):
        waits = list(waits)
        a = self.auto.get(eng)
        if a is not None and a.n > 0:
            if eng != "tensor":
                waits.append((a, a.n))
            elif mmode is None or mmode != self.last_mmode:
                waits.append((a, a.n))
        if eng == "tensor":
            self.last_mmode = mmode if mmode is not None else ("__drain__", id(build))
        for (s, v) in waits:
            if v is not None:
                assert v <= s.n, f"wait target {v} > reserved {s.n} (future dep!)"
        seen = {}
        for (s, v) in waits:
            if v is not None and v > 0:
                if s.h.num not in seen or seen[s.h.num][1] < v:
                    seen[s.h.num] = (s.h, v)
        w = list(seen.values())
        tgt = None
        ih = None
        if inc is not None:
            tgt = inc.bump(inck)
            ih = inc.h

        def f(E, w=w, build=build, ih=ih, inck=inck):
            for h, v in w:
                E.wait_ge(h, v)
            ins = build(E)
            if ih is not None:
                assert ins is not None
                ins.then_inc(ih, inck)
        f._scope = self.cur_scope
        self.q[eng].append(f)
        return tgt

    def fence(self, eng, waits):
        ws = [(s, t) for (s, t) in waits if t]

        def f(E, ws=ws):
            for s, t in ws:
                E.wait_ge(s.h, t)
        f._scope = self.cur_scope
        self.q[eng].append(f)

    def finalize(self):
        nc = self.nc
        with nc.Block() as blk:
            for e in self.ENGS:
                fns = self.q[e]
                if not fns:
                    continue

                def runner(E, fns=fns):
                    for f in fns:
                        sc = getattr(f, "_scope", None)
                        if sc:
                            with nc.named_scope(sc):
                                f(E)
                        else:
                            f(E)
                getattr(blk, e)(runner)


def np_bf16(x):
    return np.asarray(x, dtype=BF)


def build_phase2(em, nc, xn3_d, xmid_d, w_d, b_d, ic_d, out_d, ident_d, alloc,
                 pbank, ptok, start_waits=(), s_done=None):
    s_ld = em.sem("p2_ld")
    s_xn2 = [em.sem("p2_xn0"), em.sem("p2_xn1")]
    s_mm = em.sem("p2_mm")
    s_ev = em.sem("p2_ev")
    s_v = em.sem("p2_v")
    em.set_auto("tensor", s_mm)
    em.set_auto("vector", s_v)
    em.set_auto("scalar", s_ev)
    s_xm2 = [em.sem("p2_xm0"), em.sem("p2_xm1")]
    s_st2 = [em.sem("p2_st0"), em.sem("p2_st1")]
    GB = len(pbank)

    WN = ("wf", "wi", "wc", "uf", "ui", "uc", "wo", "uo")
    W = {n: alloc(f"p2_{n}", [128, 8192], BF16) for n in WN}
    xn = [alloc(f"p2_xn{i}", [128, 4096], BF16) for i in range(2)]
    c32 = alloc("p2_c32", [128, 4096], F32)
    cbf = [alloc(f"p2_cbf{i}", [128, 4096], BF16) for i in range(2)]
    gates = {g: [alloc(f"p2_g_{g}{par}", [128, 512], F32) for par in range(2)]
             for g in "fic"}
    tmp1 = [alloc(f"p2_tmp1_{par}", [128, 512], F32) for par in range(2)]
    bcol = {n: alloc(f"p2_b_{n}", [128, 8], F32) for n in ("bf", "bi", "bc", "bo")}
    bv = {n: alloc(f"p2_bv_{n}", [128, 8], F32) for n in ("bf", "bi", "bc")}
    iccol = alloc("p2_ic", [128, 8], F32)
    iccol_bf = alloc("p2_icbf", [128, 8], BF16)
    identb = alloc("p2_iden", [128, 128], BF16)
    xmt = [alloc(f"p2_xmt{i}", [128, 1024], F32) for i in range(2)]
    sq_scr = alloc("p2_sqscr", [128, 1024], F32)
    cols = [alloc(f"p2_cols{i}", [128, 6], F32) for i in range(4)]
    cc = alloc("p2_cc", [128, 2], F32)
    _a = lambda h: nc.lookup_mloc(h).addr
    hb = nc.alloc_sbuf_tensor_at("p2_h", [128, 32768], BF16, offset=_a(W["wf"]))
    cfuo = nc.alloc_sbuf_tensor_at("p2_cfuo", [128, 4096], F32, offset=_a(W["ui"]))
    ott = [nc.alloc_sbuf_tensor_at(f"p2_ott{i}", [128, 1024], F32,
                                   offset=_a(W["uc"]) + i * 4096) for i in range(4)]

    for n in WN:
        em.op("gpsimd", lambda E, s=w_d[n], d=W[n]: E.dma_start(
            out=bass.AP(d, 0, [[8192, 128], [1, 8192]]),
            in_=bass.AP(s, 0, [[1024, 128], [128 * 1024, 8], [1, 1024]])),
            waits=start_waits, inc=s_ld, inck=16)
    for n in ("bf", "bi", "bc", "bo"):
        em.op("gpsimd", lambda E, s=b_d[n], d=bcol[n]: E.dma_start(
            out=d[:, :], in_=bass.AP(s, 0, [[1, 128], [128, 8]]),
            allow_slow_non_contiguous=True),
            waits=start_waits, inc=s_ld, inck=16)
    em.op("gpsimd", lambda E: nc.gpsimd.dma_start(
        out=iccol[:, :], in_=bass.AP(ic_d, 0, [[1, 128], [128, 8]]),
        allow_slow_non_contiguous=True),
        waits=start_waits, inc=s_ld, inck=16)
    em.op("gpsimd", lambda E: nc.gpsimd.dma_start(
        out=identb[:, :], in_=ident_d.ap()), waits=start_waits, inc=s_ld, inck=16)
    lds_done = s_ld.n
    em.op("vector", lambda E: E.memset(cc[:, 0:1], 1e-5), inc=s_v)
    em.op("vector", lambda E: E.memset(cc[:, 1:2], -0.5), inc=s_v)
    icb_t = em.op("vector", lambda E: E.tensor_copy(iccol_bf[:, :], iccol[:, :]),
                  waits=[(s_ld, lds_done)], inc=s_v)

    def xn_load(t, buf, waits):
        tg = None
        for kc in range(8):
            src = bass.AP(xn3_d, t * RPC * H + kc * 128, [[H, RPC], [1, 128]])
            tg = em.op("sync", lambda E, s=src, kc=kc, buf=buf: E.dma_start(
                out=xn[buf][:, kc * 512:(kc + 1) * 512], in_=s, transpose=True),
                waits=waits, inc=s_xn2[buf], inck=16)
        return tg

    xn_ready = [xn_load(0, 0, ()), None]
    xn_ready[1] = xn_load(1, 1, ())

    for gi, (un, bn) in enumerate(zip(("uf", "ui", "uc"), ("bf", "bi", "bc"))):
        bank = pbank[gi]
        tmm = None
        for mc in range(8):
            for kc in range(8):
                tmm = em.op("tensor", lambda E, un=un, kc=kc, mc=mc, bank=bank: E.matmul(
                    bank[:, mc:mc + 1],
                    W[un][:, kc * 1024 + mc * 128: kc * 1024 + (mc + 1) * 128],
                    iccol_bf[:, kc:kc + 1], start=(kc == 0), stop=(kc == 7)),
                    waits=([(s_ld, lds_done), (s_v, icb_t)]
                           if (mc == 0 and kc == 0 and gi == 0) else ()),
                    inc=(s_mm if (mc == 7 and kc == 7) else None), mmode="f")
        em.op("vector", lambda E, bank=bank, bn=bn: E.tensor_add(
            bv[bn][:, :], bank[:, 0:8], bcol[bn][:, :]),
            waits=[(s_mm, tmm)], inc=s_v)
    v_done = s_v.n

    bank_guard = [0] * GB
    gate_guard = {g: [0, 0] for g in "fic"}
    cast_done = 0
    bank_i = 0

    for t in range(TT):
        cur = t % 2
        cbf_r = cbf[t % 2]
        cbf_w = cbf[(t + 1) % 2]
        for mc in range(8):
            par = mc % 2
            gev = []
            for g, wn, un, bn in zip("fic", ("wf", "wi", "wc"),
                                     ("uf", "ui", "uc"), ("bf", "bi", "bc")):
                b = bank_i % GB
                bank_i += 1
                bank = pbank[b]
                waits = [(s_ev, bank_guard[b])]
                if mc == 0 and g == "f":
                    waits.append((s_xn2[cur], xn_ready[cur]))
                    waits.append((s_v, cast_done if t >= 1 else v_done))
                last_x = (t == 0)
                tmm = None
                for kc in range(8):
                    tmm = em.op("tensor", lambda E, wn=wn, kc=kc, mc=mc, bank=bank,
                          cur=cur, st=(kc == 0), sp=(last_x and kc == 7): E.matmul(
                              bank[:, 0:512],
                              W[wn][:, kc * 1024 + mc * 128: kc * 1024 + (mc + 1) * 128],
                              xn[cur][:, kc * 512:(kc + 1) * 512],
                              start=st, stop=sp),
                          waits=(waits if kc == 0 else ()),
                          inc=(s_mm if (last_x and kc == 7) else None),
                          mmode="f")
                if t >= 1:
                    for kc in range(8):
                        tmm = em.op("tensor", lambda E, un=un, kc=kc, mc=mc, bank=bank,
                              cbf_r=cbf_r: E.matmul(
                                  bank[:, 0:512],
                                  W[un][:, kc * 1024 + mc * 128: kc * 1024 + (mc + 1) * 128],
                                  cbf_r[:, kc * 512:(kc + 1) * 512],
                                  start=False, stop=(kc == 7)),
                              inc=(s_mm if kc == 7 else None), mmode="f")
                func = AF.Tanh if g == "c" else AF.Sigmoid
                bias = bv[bn] if t == 0 else bcol[bn]
                gt_t = gates[g][par]
                ev = em.op("scalar", lambda E, gt_t=gt_t, bank=bank, func=func,
                           bias=bias, mc=mc: E.activation(
                               gt_t[:, :], bank[:, 0:512], func,
                               bias=bias[:, mc:mc + 1]),
                           waits=[(s_mm, tmm), (s_v, gate_guard[g][par])],
                           inc=s_ev)
                bank_guard[b] = ev
                gev.append(ev)
            sl = slice(mc * 512, (mc + 1) * 512)
            wv = [(s_ev, max(gev))]
            if t == 0:
                em.op("vector", lambda E, par=par, mc=mc: E.tensor_scalar(
                    tmp1[par][:, :], gates["f"][par][:, :], iccol[:, mc:mc + 1],
                    None, ALU.mult), waits=wv, inc=s_v)
                em.op("vector", lambda E, sl=sl, par=par: E.tensor_mul(
                    c32[:, sl], gates["i"][par][:, :], gates["c"][par][:, :]),
                    inc=s_v)
                em.op("vector", lambda E, sl=sl, par=par: E.tensor_add(
                    c32[:, sl], c32[:, sl], tmp1[par][:, :]), inc=s_v)
            else:
                em.op("vector", lambda E, par=par: E.tensor_mul(
                    tmp1[par][:, :], gates["i"][par][:, :], gates["c"][par][:, :]),
                    waits=wv, inc=s_v)
                em.op("vector", lambda E, sl=sl, par=par: E.tensor_mul(
                    c32[:, sl], gates["f"][par][:, :], c32[:, sl]), inc=s_v)
                em.op("vector", lambda E, sl=sl, par=par: E.tensor_add(
                    c32[:, sl], c32[:, sl], tmp1[par][:, :]), inc=s_v)
            gg = em.op("vector", lambda E, sl=sl, cbf_w=cbf_w: E.tensor_copy(
                cbf_w[:, sl], c32[:, sl]), inc=s_v)
            for g in "fic":
                gate_guard[g][par] = gg
            if mc == 7:
                cast_done = gg
        if t + 2 < TT:
            xn_ready[cur] = xn_load(t + 2, cur, [(s_mm, s_mm.n)])

    cbf_f = cbf[TT % 2]
    for mc in range(8):
        b = bank_i % GB
        bank_i += 1
        bank = pbank[b]
        waits = [(s_ev, bank_guard[b])]
        if mc == 0:
            waits.append((s_v, cast_done))
        tmm = None
        for kc in range(8):
            tmm = em.op("tensor", lambda E, kc=kc, mc=mc, bank=bank: E.matmul(
                bank[:, 0:512],
                W["uo"][:, kc * 1024 + mc * 128: kc * 1024 + (mc + 1) * 128],
                cbf_f[:, kc * 512:(kc + 1) * 512],
                start=(kc == 0), stop=(kc == 7)),
                waits=(waits if kc == 0 else ()),
                inc=(s_mm if kc == 7 else None), mmode="f")
        ev = em.op("scalar", lambda E, bank=bank, mc=mc: E.activation(
            cfuo[:, mc * 512:(mc + 1) * 512], bank[:, 0:512], AF.Identity,
            bias=bcol["bo"][:, mc:mc + 1]), waits=[(s_mm, tmm)], inc=s_ev)
        bank_guard[b] = ev

    tmp_guard = [0, 0]
    for t in range(TT):
        cur = t % 2
        xn_ready[cur] = xn_load(t, cur, [(s_mm, s_mm.n)])
        for mc in range(8):
            b = bank_i % GB
            bank_i += 1
            bank = pbank[b]
            waits = [(s_ev, bank_guard[b])]
            if mc == 0:
                waits.append((s_xn2[cur], xn_ready[cur]))
            tmm = None
            for kc in range(8):
                tmm = em.op("tensor", lambda E, kc=kc, mc=mc, bank=bank, cur=cur: E.matmul(
                    bank[:, 0:512],
                    W["wo"][:, kc * 1024 + mc * 128: kc * 1024 + (mc + 1) * 128],
                    xn[cur][:, kc * 512:(kc + 1) * 512],
                    start=(kc == 0), stop=(kc == 7)),
                    waits=(waits if kc == 0 else ()),
                    inc=(s_mm if kc == 7 else None), mmode="f")
            par = mc % 2
            tv = em.op("vector", lambda E, bank=bank, mc=mc, par=par: E.tensor_add(
                tmp1[par][:, :], bank[:, 0:512], cfuo[:, mc * 512:(mc + 1) * 512]),
                waits=[(s_mm, tmm), (s_ev, tmp_guard[par])], inc=s_v)
            ev = em.op("scalar", lambda E, t=t, mc=mc, par=par: E.activation(
                hb[:, t * 4096 + mc * 512: t * 4096 + (mc + 1) * 512],
                tmp1[par][:, :], AF.Tanh), waits=[(s_v, tv)], inc=s_ev)
            bank_guard[b] = ev
            tmp_guard[par] = ev

    stageb_done = s_ev.n
    _a2 = lambda h: nc.lookup_mloc(h).addr
    hbT = [nc.alloc_sbuf_tensor_at(f"p2_hbT{i}", [128, 1024], BF16,
                                   offset=_a2(W["ui"]) + i * 2048)
           for i in range(8)]
    xmt8 = ([nc.alloc_sbuf_tensor_at(f"p2_xmt8_{i}", [128, 1024], F32,
                                     offset=_a2(W["uo"]) + i * 4096)
             for i in range(4)] +
            [nc.alloc_sbuf_tensor_at(f"p2_xmt8_{4 + i}", [128, 1024], F32,
                                     offset=_a2(W["wo"]) + i * 4096)
             for i in range(4)])
    rn2 = [nc.alloc_sbuf_tensor_at(f"p2_rn{i}", [128, 1024], F32,
                                   offset=_a2(W["wf"]) + i * 4096)
           for i in range(2)]
    stq = [alloc(f"p2_stq{i}", [128, 16], F32) for i in range(2)]
    s_xm4 = [em.sem(f"p2_xm4_{i}") for i in range(8)]
    s_st4 = [em.sem(f"p2_st4_{i}") for i in range(4)]
    cp_gd = [0, 0]
    stt_gd = [0] * 8
    rn_gd2 = [0, 0]
    xl_gd = [0] * 8
    xltgt = [0] * 8
    ot_gd = [0] * 4
    nmq_gd = [0, 0]
    nmq = {}

    def c_stageA(b):
        bp = b % 2
        stn = stq[bp]
        for rc in range(4):
            pp = rc % 2
            ptile = ptok[pp]
            hs = (bp * 4) + rc
            tmm = None
            for mcc in range(8):
                tmm = em.op("tensor", lambda E, b=b, rc=rc, mcc=mcc,
                            ptile=ptile: E.transpose(
                    ptile[:, mcc * 128:(mcc + 1) * 128],
                    bass.AP(hb, b * 4096 + mcc * 512 + rc * 128,
                            [[32768, 128], [1, 128]]),
                    identb[:, :]),
                    waits=([(s_ev, stageb_done), (s_v, cp_gd[pp])]
                           if mcc == 0 else ()),
                    inc=(s_mm if mcc == 7 else None), mmode="t")
            cp = em.op("vector", lambda E, hs=hs, ptile=ptile: E.tensor_copy(
                hbT[hs][:, :], ptile[:, :]),
                waits=[(s_mm, tmm), (s_v, stt_gd[hs])], inc=s_v)
            cp_gd[pp] = cp
            em.op("vector", lambda E, hs=hs, stn=stn, rc=rc: E.reduce_sum(
                stn[:, rc:rc + 1], hbT[hs][:, :], axis=mybir.AxisListType.X),
                inc=s_v)
            sq = em.op("scalar", lambda E, hs=hs, stn=stn, rc=rc: E.activation(
                sq_scr[:, :], hbT[hs][:, :], AF.Square,
                accum_out=stn[:, 4 + rc:5 + rc]),
                waits=[(s_v, s_v.n)], inc=s_ev)
        for rc in range(4):
            xs = (b % 2) * 4 + rc
            xltgt[xs] = em.op("sync", lambda E, b=b, rc=rc, xs=xs: E.dma_start(
                out=xmt8[xs][:, :],
                in_=bass.AP(xmid_d, (b * RPC + rc * 128) * H, [[H, 128], [1, H]])),
                waits=([(s_ev, stageb_done)] if b < 2 else
                       [(s_v, xl_gd[xs])]), inc=s_xm4[xs], inck=16)
        return sq

    def c_chain(b, sq_t):
        bp = b % 2
        stn = stq[bp]
        em.op("vector", lambda E, stn=stn: E.tensor_scalar(
            stn[:, 0:4], stn[:, 0:4], 1.0 / H, None, ALU.mult), inc=s_v)
        em.op("vector", lambda E, stn=stn: E.tensor_mul(
            stn[:, 12:16], stn[:, 0:4], stn[:, 0:4]), inc=s_v)
        vv = em.op("vector", lambda E, stn=stn: E.scalar_tensor_tensor(
            stn[:, 8:12], stn[:, 4:8], 1.0 / H, stn[:, 12:16],
            ALU.mult, ALU.subtract), waits=[(s_ev, sq_t)], inc=s_v)
        em.op("scalar", lambda E, stn=stn: E.activation(
            stn[:, 8:12], stn[:, 8:12], AF.Ln, bias=cc[:, 0:1]),
            waits=[(s_v, vv)], inc=s_ev)
        ex = em.op("scalar", lambda E, stn=stn: E.activation(
            stn[:, 8:12], stn[:, 8:12], AF.Exp, scale=cc[:, 1:2]), inc=s_ev)
        nm = em.op("vector", lambda E, stn=stn: E.scalar_tensor_tensor(
            stn[:, 12:16], stn[:, 0:4], -1.0, stn[:, 8:12],
            ALU.mult, ALU.mult), waits=[(s_ev, ex)], inc=s_v)
        nmq_gd[bp] = nm
        return nm

    def c_stageB(b, nm_t):
        bp = b % 2
        stn = stq[bp]
        for rc in range(4):
            hs = (bp * 4) + rc
            ot = ott[rc]
            xs = (b % 2) * 4 + rc
            rb = em.op("vector", lambda E, rc=rc, xs=xs, stn=stn: E.tensor_scalar(
                rn2[rc % 2][:, :], xmt8[xs][:, :], stn[:, 12 + rc:13 + rc],
                None, ALU.add),
                waits=[(s_v, max(nm_t, rn_gd2[rc % 2])),
                       (s_xm4[xs], xltgt[xs])], inc=s_v)
            xl_gd[xs] = rb
            stt = em.op("vector", lambda E, hs=hs, rc=rc, stn=stn, ot=ot:
                        E.scalar_tensor_tensor(
                            ot[:, :], hbT[hs][:, :], stn[:, 8 + rc:9 + rc],
                            rn2[rc % 2][:, :], ALU.mult, ALU.add),
                waits=[(s_st4[rc], ot_gd[rc])], inc=s_v)
            stt_gd[hs] = stt
            rn_gd2[rc % 2] = stt
            ot_gd[rc] = em.op("sync", lambda E, b=b, rc=rc, ot=ot: E.dma_start(
                out=bass.AP(out_d, (b * RPC + rc * 128) * H, [[H, 128], [1, H]]),
                in_=ot[:, :]), waits=[(s_v, stt)], inc=s_st4[rc], inck=16)

    sqq = {}
    nmm = {}
    for blk in range(9):
        if blk < TT:
            sqq[blk] = c_stageA(blk)
            nmm[blk] = c_chain(blk, sqq[blk])
        if blk >= 1:
            c_stageB(blk - 1, nmm[blk - 1])
    return s_st4


def make_consts():
    c = {}
    a2 = np.arange(32)
    k1b = np.arange(32)
    t2b = np.arange(128)
    k2b = np.arange(64)
    t1 = np.arange(128)
    t2i = np.arange(32)

    def bd4(m):
        return np.kron(np.eye(4), m)

    thE = 2 * np.pi * np.outer(a2, k1b) / 32.0
    thO = thE + (np.pi * a2 / 32.0)[:, None]
    c["W1"] = {("E", "re"): bd4(np.cos(thE)), ("E", "im"): bd4(-np.sin(thE)),
               ("O", "re"): bd4(np.cos(thO)), ("O", "im"): bd4(-np.sin(thO))}

    P2 = {}
    for path in "EO":
        arrs = np.zeros((32, 4, 128, 64))
        for k in range(32):
            A = 2 * np.pi * np.outer(t2b, k + 32 * k2b) / 4096.0
            if path == "O":
                A = A + (np.pi * t2b / 4096.0)[:, None]
            arrs[k, 0] = np.cos(A)
            arrs[k, 1] = np.sin(A)
            arrs[k, 2] = -np.sin(A)
            arrs[k, 3] = np.cos(A)
        if path == "E":
            arrs[0, 2][:, 0] = (-1.0) ** t2b
            arrs[0, 3][:, 0] = 0.0
        P2[path] = arrs.transpose(2, 0, 1, 3).reshape(128, 32 * 4 * 64)
    c["P2"] = P2

    I1 = {}
    for path in "EO":
        arrs = np.zeros((32, 4, 64, 128))
        for k in range(32):
            psi = 2 * np.pi * (np.outer(k2b, t1) / 128.0 + k * t1[None, :] / 4096.0)
            if path == "O":
                psi = psi + 2 * np.pi * t1[None, :] / 8192.0
            arrs[k, 0] = np.cos(psi)
            arrs[k, 1] = -np.sin(psi)
            arrs[k, 2] = np.sin(psi)
            arrs[k, 3] = np.cos(psi)
        if path == "E":
            arrs[0, 1][0, :] = (-1.0) ** t1
            arrs[0, 3][0, :] = 0.0
        packed = np.zeros((128, 16 * 4 * 128))
        for kh in range(16):
            for kind in range(4):
                packed[0:64, (kh * 4 + kind) * 128:(kh * 4 + kind + 1) * 128] = \
                    arrs[2 * kh, kind]
                packed[64:128, (kh * 4 + kind) * 128:(kh * 4 + kind + 1) * 128] = \
                    arrs[2 * kh + 1, kind]
        I1[path] = packed
    c["I1"] = I1

    I2 = {}
    for path in "EO":
        p2 = 2 * np.pi * np.outer(k1b, t2i) / 32.0
        if path == "O":
            p2 = p2 + 2 * np.pi * t2i[None, :] / 64.0
        a = np.zeros((128, 128))
        bb = np.zeros((128, 128))
        for kk in range(32):
            for qq in range(4):
                a[4 * kk + qq, 32 * qq:32 * qq + 32] = np.cos(p2)[kk]
                bb[4 * kk + qq, 32 * qq:32 * qq + 32] = -np.sin(p2)[kk]
        I2[(path, "a")] = a
        I2[(path, "b")] = bb
    c["I2"] = I2

    c["ident"] = np.eye(128)
    return c


def load_consts(em, nc, con, alloc, s_ld, subset=None, pref=""):
    sb = {}

    def put(name, arr):
        name = pref + name
        arrb = np.asarray(arr, dtype=BF)
        d = nc.inline_tensor(np.ascontiguousarray(arrb), name + "_c")
        h = alloc(name, list(arrb.shape), BF16)
        em.op("sync", lambda E, d=d, h=h: E.dma_start(out=h[:, :], in_=d.ap()),
              inc=s_ld, inck=16)
        return h

    def putf(name, arr):
        name = pref + name
        arrc = np.ascontiguousarray(np.asarray(arr, dtype=np.float32))
        d = nc.inline_tensor(arrc, name + "_c")
        h = alloc(name, list(arrc.shape), F32)
        em.op("sync", lambda E, d=d, h=h: E.dma_start(out=h[:, :], in_=d.ap()),
              inc=s_ld, inck=16)
        return h

    sb_extra_loaded = False
    if subset != "constr":
        sb_extra = {}
        sb_extra["selA"] = putf("selAf", np.ones((1, 128), np.float32))
        sb_extra["eps_c"] = putf("epsc", np.full((128, 1), 1e-5, np.float32))
        sb_extra["neghalf_c"] = putf("neghalfc", np.full((128, 1), -0.5, np.float32))
        sb.update(sb_extra)
    if subset == "constr":
        for k, v in con["W1"].items():
            if k[0] == "O":
                sb[("W1",) + k] = put(f"W1{k[0]}{k[1]}", v)
        sb[("P2", "O")] = put("P2O", con["P2"]["O"])
        sb["ident"] = put("identc", con["ident"])
        return sb
    for k, v in con["W1"].items():
        sb[("W1",) + k] = put(f"W1{k[0]}{k[1]}", v)
    for p in "EO":
        sb[("P2", p)] = put(f"P2{p}", con["P2"][p])
        sb[("I1", p)] = put(f"I1{p}", con["I1"][p])
    for k, v in con["I2"].items():
        sb[("I2",) + k] = put(f"I2{k[0]}{k[1]}", v)
    sb["ident"] = put("identc", con["ident"])
    return sb


def build_conv_bufs(nc, alloc):
    b = {}
    b["u"] = [alloc(f"cv_u{i}", [128, 4096], BF16) for i in range(2)]
    b["X1"] = alloc("cv_X1", [128, 4096], BF16)
    b["X1T"] = {k: alloc(f"cv_X1T{k[0]}{k[1]}", [128, 4096], BF16)
                for k in (("E", "re"), ("E", "im"), ("O", "re"), ("O", "im"))}
    b["S"] = {k: alloc(f"cv_S{k[0]}{k[1]}", [128, 2048], BF16)
              for k in (("E", "re"), ("E", "im"), ("O", "re"), ("O", "im"))}
    b["P"] = {k: alloc(f"cv_P{k[0]}{k[1]}", [128, 2048], BF16)
              for k in (("E", "re"), ("E", "im"), ("O", "re"), ("O", "im"))}
    b["K"] = {k: alloc(f"cv_K{k[0]}{k[1]}", [128, 2048], BF16)
              for k in (("E", "re"), ("E", "im"), ("O", "re"), ("O", "im"))}
    b["Y"] = [alloc(f"cv_Y{i}", [128, 4096], BF16) for i in range(2)]
    import concourse.bass as _bass
    nc0 = b["X1T"][("E", "re")]
    b["YT"] = {k: b["X1T"][k] for k in b["X1T"]}
    b["h1g"] = alloc("cv_h1g", [128, 4096], BF16)
    b["cm1"] = alloc("cv_cm1", [128, 2048], BF16)
    b["cm2"] = b["cm1"]
    return b


def build_conv_group(em, nc, sb, b, pb, pt, sems, guards, u_ready, k_ready):
    s_mm, s_v, s_ev, s_u, s_k = sems
    u = b["u"][guards["upar"]]
    planes4 = [("E", "re"), ("E", "im"), ("O", "re"), ("O", "im")]
    bg = guards["bg"]

    for pi, (path, pl) in enumerate(planes4):
        W = sb[("W1", path, pl)]
        for ch in range(8):
            bi = ch % 2
            mm = em.op("tensor", lambda E, W=W, ch=ch, bank=pb[bi]: E.matmul(
                bank[:, 0:512], W[:, :], u[:, ch * 512:(ch + 1) * 512],
                start=True, stop=True),
                waits=[(s_u, u_ready), (s_ev, bg[bi])], inc=s_mm, mmode="f")
            ev = em.op("scalar", lambda E, ch=ch, bank=pb[bi]: E.activation(
                b["X1"][:, ch * 512:(ch + 1) * 512], bank[:, 0:512], AF.Identity),
                waits=[(s_mm, mm), (s_v, guards["x1"])], inc=s_ev)
            bg[bi] = ev
        x1_done = s_ev.n
        dstT = b["X1T"][(path, pl)]
        for cb in range(4):
            ti = cb % 2
            mm = None
            for cc in range(8):
                cidx = cb * 8 + cc
                mm = em.op("tensor", lambda E, cidx=cidx, cc=cc, bank=pt[ti]: E.transpose(
                    bank[:, cc * 128:(cc + 1) * 128],
                    b["X1"][:, cidx * 128:(cidx + 1) * 128], sb["ident"][:, :]),
                    waits=([(s_v, guards["pt"][ti]), (s_ev, x1_done)]
                           if cc == 0 else []),
                    inc=(s_mm if cc == 7 else None), mmode="t")
            cp = em.op("vector", lambda E, cb=cb, bank=pt[ti], dstT=dstT: E.tensor_copy(
                dstT[:, cb * 1024:(cb + 1) * 1024], bank[:, :]),
                waits=[(s_mm, mm), (s_mm, guards["x1t"])], inc=s_v)
            guards["pt"][ti] = cp
        guards["x1"] = s_v.n

    turns_done = s_v.n
    first_p2 = True
    for path in "EO":
        for kh in range(16):
            bA = pb[2 + (kh % 2) * 2]
            bB = pb[3 + (kh % 2) * 2]
            mm = None
            for member in range(2):
                k1 = 2 * kh + member
                ro = 64 * member
                rhs_re = bass.AP(b["X1T"][(path, "re")], k1,
                                 [[4096, 128], [32, 128]])
                rhs_im = bass.AP(b["X1T"][(path, "im")], k1,
                                 [[4096, 128], [32, 128]])
                for kind, (bank, rr) in enumerate(
                        ((bA, rhs_re), (bA, rhs_im), (bB, rhs_re), (bB, rhs_im))):
                    st = sb[("P2", path)][:, (k1 * 4 + kind) * 64:(k1 * 4 + kind + 1) * 64]
                    mm = em.op("tensor", lambda E, st=st, rr=rr, bank=bank, ro=ro,
                               kind=kind: E.matmul(
                                   bank[ro:ro + 64, 0:128], st, rr,
                                   start=(kind % 2 == 0), stop=(kind % 2 == 1),
                                   tile_position=(0, ro)),
                               waits=(([(s_ev, bg[2 + (kh % 2) * 2]),
                                        (s_ev, bg[3 + (kh % 2) * 2])] +
                                       ([(s_v, turns_done)] if first_p2 else []))
                                      if (member == 0 and kind == 0) else []),
                               inc=(s_mm if (member == 1 and kind == 3) else None),
                               mmode="c")
            for bank, pl in ((bA, "re"), (bB, "im")):
                ev = em.op("scalar", lambda E, bank=bank, pl=pl, kh=kh, path=path:
                           E.activation(
                               bass.AP(b["S"][(path, pl)], kh * 128,
                                       [[2048, 128], [1, 32], [32, 4]]),
                               bank[0:128, 0:128], AF.Identity),
                           waits=[(s_mm, mm), (s_v, guards["s_free"])], inc=s_ev)
            bg[2 + (kh % 2) * 2] = ev
            bg[3 + (kh % 2) * 2] = ev
            first_p2 = False
    guards["x1t"] = s_mm.n
    p2_done = s_ev.n

    last = 0
    for path in "EO":
        Sre, Sim = b["S"][(path, "re")], b["S"][(path, "im")]
        Kre, Kim = b["K"][(path, "re")], b["K"][(path, "im")]
        Pre, Pim = b["P"][(path, "re")], b["P"][(path, "im")]
        t1b = b["cm1"]
        em.op("vector", lambda E, a=Sre, c=Kre, o=t1b: E.tensor_mul(
            o[:, :], a[:, :], c[:, :]),
            waits=[(s_ev, p2_done), (s_k, k_ready), (s_mm, guards["p_free"])],
            inc=s_v)
        em.op("vector", lambda E, a=Sim, c=Kim, o=Pre: E.tensor_mul(
            o[:, :], a[:, :], c[:, :]), inc=s_v)
        em.op("vector", lambda E, a=t1b, o=Pre: E.tensor_sub(
            o[:, :], a[:, :], o[:, :]), inc=s_v)
        em.op("vector", lambda E, a=Sre, c=Kim, o=t1b: E.tensor_mul(
            o[:, :], a[:, :], c[:, :]), inc=s_v)
        em.op("vector", lambda E, a=Sim, c=Kre, o=Pim: E.tensor_mul(
            o[:, :], a[:, :], c[:, :]), inc=s_v)
        last = em.op("vector", lambda E, a=t1b, o=Pim: E.tensor_add(
            o[:, :], a[:, :], o[:, :]), inc=s_v)
        if path == "E":
            em.op("vector", lambda E, Sre=Sre, Kre=Kre, Pre=Pre: E.tensor_mul(
                Pre[0:1, 0:128], Sre[0:1, 0:128], Kre[0:1, 0:128]), inc=s_v)
            last = em.op("vector", lambda E, Sim=Sim, Kim=Kim, Pim=Pim: E.tensor_mul(
                Pim[0:1, 0:128], Sim[0:1, 0:128], Kim[0:1, 0:128]), inc=s_v)
    cm_done = last
    guards["s_free"] = cm_done

    for pi, (path, pl) in enumerate(planes4):
        Yb = b["Y"][pi % 2]
        I1 = sb[("I1", path)]
        kinds = (0, 1) if pl == "re" else (2, 3)
        for blk in range(8):
            biA = 2 + 2 * (blk % 2)
            biB = 3 + 2 * (blk % 2)
            bkM = (pb[biA], pb[biB])
            mm = None
            for sub in range(4):
                idx = blk * 4 + sub
                kh, member = idx // 2, idx % 2
                ro = 64 * member
                bank = bkM[member]
                cslot = sub // 2
                for ki, kind in enumerate(kinds):
                    rr = b["P"][(path, "re" if kind % 2 == 0 else "im")]
                    st = I1[ro:ro + 64,
                            (kh * 4 + kind) * 128:(kh * 4 + kind + 1) * 128]
                    mm = em.op("tensor", lambda E, st=st, rr=rr, bank=bank,
                               cslot=cslot, ki=ki, ro=ro, kh=kh: E.matmul(
                                   bank[:, cslot * 128:(cslot + 1) * 128], st,
                                   rr[ro:ro + 64, kh * 128:(kh + 1) * 128],
                                   start=(ki == 0), stop=(ki == 1)),
                               waits=([(s_v, cm_done), (s_ev, bg[biA]),
                                       (s_ev, bg[biB])]
                                      if (sub == 0 and ki == 0) else []),
                               inc=(s_mm if (sub == 3 and ki == 1) else None),
                               mmode="r")
            ev = None
            for m in range(2):
                ev = em.op("scalar", lambda E, bank=bkM[m], blk=blk, m=m,
                           Yb=Yb: E.activation(
                    bass.AP(Yb, blk * 512 + m * 128, [[4096, 128], [256, 2],
                                                      [1, 128]]),
                    bank[:, 0:256], AF.Identity),
                    waits=[(s_mm, mm), (s_mm, guards["y"][pi % 2])], inc=s_ev)
            bg[biA] = ev
            bg[biB] = ev
        y_done = s_ev.n
        dstT = b["YT"][(path, pl)]
        mm = None
        for cb in range(4):
            ti = cb % 2
            for cc in range(8):
                c = cb * 8 + cc
                inap = bass.AP(Yb, c, [[4096, 128], [32, 128]])
                mm = em.op("tensor", lambda E, inap=inap, cc=cc, bank=pt[ti]: E.transpose(
                    bank[:, cc * 128:(cc + 1) * 128], inap, sb["ident"][:, :]),
                    waits=([(s_v, guards["pt"][ti]), (s_ev, y_done)]
                           if cc == 0 else []),
                    inc=(s_mm if cc == 7 else None), mmode="t")
            cp = em.op("vector", lambda E, cb=cb, bank=pt[ti], dstT=dstT: E.tensor_copy(
                dstT[:, cb * 1024:(cb + 1) * 1024], bank[:, :]),
                waits=[(s_mm, mm), (s_mm, max(guards["yt"], guards["x1t"]))], inc=s_v)
            guards["pt"][ti] = cp
        guards["y"][pi % 2] = mm

    h1_ev = 0
    for ch in range(8):
        bi = 4 + (ch % 2)
        bank = pb[bi]
        mm = None
        for ai, (path, pl, kind) in enumerate(
                (("E", "re", "a"), ("E", "im", "b"),
                 ("O", "re", "a"), ("O", "im", "b"))):
            st = sb[("I2", path, kind)]
            mm = em.op("tensor", lambda E, st=st, path=path, pl=pl, ch=ch,
                       bank=bank, ai=ai: E.matmul(
                           bank[:, 0:512], st[:, :],
                           b["YT"][(path, pl)][:, ch * 512:(ch + 1) * 512],
                           start=(ai == 0), stop=(ai == 3)),
                       waits=([(s_ev, bg[bi]), (s_v, s_v.n)] if ai == 0 else []),
                       inc=(s_mm if ai == 3 else None), mmode="f")
        h1_ev = em.op("scalar", lambda E, ch=ch, bank=bank: E.activation(
            b["h1g"][:, ch * 512:(ch + 1) * 512], bank[:, 0:512], AF.Identity),
            waits=[(s_mm, mm)] + ([guards["h1g"]] if guards["h1g"] else []),
            inc=s_ev)
        bg[bi] = h1_ev
    guards["p_free"] = s_mm.n
    guards["yt"] = s_mm.n
    guards["x1t"] = s_mm.n
    return h1_ev


def make_constr_consts():
    c = {}
    ll = np.arange(L)
    z = np.exp(-2j * np.pi * ll / L)
    G0 = 2.0 * (1 - z) / (1 + z)
    f2 = 2.0 / (1 + z)
    def rep(v):
        out = np.zeros((128, 2048), np.float32)
        out[0:64, :] = v[0:2048][None, :]
        out[64:128, :] = v[2048:4096][None, :]
        return out
    c["G0re"] = rep(G0.real)
    c["G0im"] = rep(G0.imag)
    c["f2re"] = rep(f2.real)
    c["f2im"] = rep(f2.imag)
    s0 = np.zeros((1, 128), np.float32); s0[0, 0:64] = 1.0
    s1 = np.zeros((1, 128), np.float32); s1[0, 64:128] = 1.0
    c["selA0"], c["selA1"] = s0, s1
    c["selA"] = np.ones((1, 128), np.float32)
    c["identF"] = np.eye(128, dtype=np.float32)
    t1K = np.arange(128)
    l2 = np.arange(128)
    coss = np.zeros((32, 128, 128)); sins = np.zeros((32, 128, 128))
    for k in range(32):
        Phi = 2 * np.pi * (np.outer(l2, t1K) / 128.0 + k * t1K[None, :] / 4096.0)
        coss[k] = np.cos(Phi) / 4096.0
        sins[k] = np.sin(Phi) / 4096.0
    c["KP1c"] = coss.transpose(1, 0, 2).reshape(128, 32 * 128)
    c["KP1s"] = sins.transpose(1, 0, 2).reshape(128, 32 * 128)
    c["KP1sn"] = -c["KP1s"]
    t2K = np.arange(32)
    a = np.zeros((128, 128)); bb = np.zeros((128, 128))
    for k in range(32):
        row = np.cos(2 * np.pi * k * t2K / 32.0)
        rows2 = np.sin(2 * np.pi * k * t2K / 32.0)
        for qq in range(4):
            a[4 * k + qq, 32 * qq:32 * qq + 32] = row
            bb[4 * k + qq, 32 * qq:32 * qq + 32] = -rows2
    c["KP2a"] = a
    c["KP2b"] = bb
    return c


def build_construction(em, nc, sb, cc, inp, alloc, pb, pt, sems, kplane_done, dbg=None):
    s_mm, s_v, s_ev, s_ld = sems
    F = F32
    al = alloc
    cols = al("cs_cols", [128, 16], F)
    qcols = al("cs_qcols", [128, 8], F)
    ctre = al("cs_ctre", [128, 128], F)
    ctim = al("cs_ctim", [128, 128], F)
    ctimn = al("cs_ctimn", [128, 128], F)
    g2re = al("cs_g2re", [128, 2048], F)
    g2im = al("cs_g2im", [128, 2048], F)
    mS1 = em.sbuf_mark()
    cd_re = al("cs_cdre", [128, 2048], F)
    cd_im = al("cs_cdim", [128, 2048], F)
    t1b = al("cs_t1", [128, 2048], F)
    t2b = al("cs_t2", [128, 2048], F)
    t3b = al("cs_t3", [128, 2048], F)
    w2re = al("cs_w2re", [128, 2048], F)
    t4b = w2re
    w2im = al("cs_w2im", [128, 2048], F)

    k10b = {nm: al(f"cs_{nm}", [128, 2048], F)
            for nm in ("k10re", "k10im", "k11re", "k11im")}
    mini = al("cs_mini", [1, 512], F)

    def S_(nm):
        return k10b[nm]

    for i, nm in enumerate(("diag_re", "diag_im", "B_re", "B_im",
                            "P_re", "P_im", "Q_re", "Q_im")):
        for lh in range(2):
            em.op("sync", lambda E, nm=nm, i=i, lh=lh: E.dma_start(
                out=cols[64 * lh:64 * lh + 64, i:i + 1],
                in_=bass.AP(inp[nm], 0, [[1, 64], [1, 1]]),
                allow_slow_non_contiguous=True), inc=s_ld, inck=16)
    for lh in range(2):
        for nm, dst in (("ctre", ctre), ("ctim", ctim)):
            em.op("sync", lambda E, nm=nm, dst=dst, lh=lh: E.dma_start(
                out=dst[64 * lh:64 * lh + 64, :],
                in_=bass.AP(inp[nm], 0, [[128, 64], [1, 128]])), inc=s_ld, inck=16)
    em.op("sync", lambda E: E.dma_start(
        out=cols[0:1, 8:9], in_=bass.AP(inp["step"], 0, [[1, 1], [1, 1]])),
        inc=s_ld, inck=16)
    lds = s_ld.n

    V = lambda f, **kw: em.op("vector", f, **kw)
    SC = lambda f, **kw: em.op("scalar", f, **kw)
    T = lambda f, **kw: em.op("tensor", f, **kw)

    V(lambda E: E.tensor_scalar(cols[0:1, 9:10], cols[0:1, 8:9], 1e-6, None, ALU.max),
      waits=[(s_ld, lds)], inc=s_v)
    vv = V(lambda E: E.tensor_copy(cols[0:1, 10:11], cols[0:1, 9:10]), inc=s_v)
    mmb = T(lambda E: E.matmul(pb[0][:, 0:1], cc["selA"][0:1, :],
                               cols[0:1, 10:11], start=True, stop=True),
            waits=[(s_v, vv)], inc=s_mm)
    rs_b = SC(lambda E: E.activation(cols[:, 11:12], pb[0][:, 0:1], AF.Identity),
              waits=[(s_mm, mmb)], inc=s_ev)

    V(lambda E: E.tensor_mul(cols[:, 12:13], cols[:, 11:12], cols[:, 0:1]),
      waits=[(s_ev, rs_b)], inc=s_v)
    V(lambda E: E.tensor_mul(cols[:, 13:14], cols[:, 11:12], cols[:, 1:2]),
      inc=s_v)
    V(lambda E: E.tensor_scalar(t1b[:, :], cc["G0re"][:, :], cols[:, 12:13],
                                None, ALU.subtract), inc=s_v)
    V(lambda E: E.tensor_scalar(t2b[:, :], cc["G0im"][:, :], cols[:, 13:14],
                                None, ALU.subtract), inc=s_v)
    V(lambda E: E.tensor_mul(t3b[:, :], t1b[:, :], t1b[:, :]), inc=s_v)
    V(lambda E: E.tensor_mul(t4b[:, :], t2b[:, :], t2b[:, :]), inc=s_v)
    V(lambda E: E.tensor_add(t3b[:, :], t3b[:, :], t4b[:, :]), inc=s_v)
    V(lambda E: E.reciprocal(t4b[:, :], t3b[:, :]), inc=s_v)
    V(lambda E: E.tensor_mul(cd_re[:, :], t1b[:, :], t4b[:, :]), inc=s_v)
    V(lambda E: E.tensor_scalar(cd_re[:, :], cd_re[:, :], cols[:, 11:12],
                                None, ALU.mult), inc=s_v)
    V(lambda E: E.scalar_tensor_tensor(cd_im[:, :], t2b[:, :], -1.0, t4b[:, :],
                                       ALU.mult, ALU.mult), inc=s_v)
    V(lambda E: E.tensor_scalar(cd_im[:, :], cd_im[:, :], cols[:, 11:12],
                                None, ALU.mult), inc=s_v)
    def cmul_cols(dr, di, ar, ai, br, bi):
        V(lambda E: E.tensor_mul(qcols[:, dr:dr + 1], cols[:, ar:ar + 1],
                                 cols[:, br:br + 1]), inc=s_v)
        V(lambda E: E.tensor_mul(cols[:, 12:13], cols[:, ai:ai + 1],
                                 cols[:, bi:bi + 1]), inc=s_v)
        V(lambda E: E.tensor_sub(qcols[:, dr:dr + 1], qcols[:, dr:dr + 1],
                                 cols[:, 12:13]), inc=s_v)
        V(lambda E: E.tensor_mul(qcols[:, di:di + 1], cols[:, ar:ar + 1],
                                 cols[:, bi:bi + 1]), inc=s_v)
        V(lambda E: E.tensor_mul(cols[:, 12:13], cols[:, ai:ai + 1],
                                 cols[:, br:br + 1]), inc=s_v)
        V(lambda E: E.tensor_add(qcols[:, di:di + 1], qcols[:, di:di + 1],
                                 cols[:, 12:13]), inc=s_v)
    cmul_cols(0, 1, 6, 7, 2, 3)
    cmul_cols(2, 3, 6, 7, 4, 5)
    V(lambda E: E.tensor_scalar(qcols[:, 4:5], qcols[:, 1:2], -1.0, None, ALU.mult),
      inc=s_v)
    qv = V(lambda E: E.tensor_scalar(qcols[:, 5:6], qcols[:, 3:4], -1.0, None,
                                     ALU.mult), inc=s_v)

    quant = [("k10re", 0, 4), ("k10im", 1, 0), ("k11re", 2, 5), ("k11im", 3, 2)]
    last_ev = 0
    last_bc = 0
    for qi, (nm, ca, cb2) in enumerate(quant):
        for lhd in range(2):
            for chk in range(4):
                bank = pb[0]
                mm = None
                for acc, (colx, srcb) in enumerate(((ca, cd_re), (cb2, cd_im))):
                    mm = em.op("tensor", lambda E, colx=colx, srcb=srcb, lhd=lhd,
                               chk=chk, bank=bank, acc=acc: E.matmul(
                                   bank[0:1, 0:512],
                                   qcols[64 * lhd:64 * lhd + 64, colx:colx + 1],
                                   srcb[64 * lhd:64 * lhd + 64,
                                        chk * 512:(chk + 1) * 512],
                                   start=(acc == 0), stop=(acc == 1)),
                               waits=([(s_v, qv), (s_ev, last_ev)]
                                      if acc == 0 else []),
                               inc=(s_mm if acc == 1 else None))
                last_ev = SC(lambda E, bank=bank: E.activation(
                    mini[0:1, 0:512], bank[0:1, 0:512], AF.Identity),
                    waits=[(s_mm, mm), (s_mm, last_bc)], inc=s_ev)
                sel = "selA0" if lhd == 0 else "selA1"
                mmb2 = em.op("tensor", lambda E, sel=sel: E.matmul(
                    pb[1][:, 0:512], cc[sel][0:1, :], mini[0:1, 0:512],
                    start=True, stop=True),
                    waits=[(s_ev, last_ev)], inc=s_mm)
                last_bc = mmb2
                rows = slice(64 * lhd, 64 * lhd + 64)
                SC(lambda E, nm=nm, chk=chk, rows=rows: E.activation(
                    k10b[nm][rows, chk * 512:(chk + 1) * 512],
                    pb[1][rows, 0:512], AF.Identity),
                   waits=[(s_mm, mmb2)], inc=s_ev)
    k10_done = s_ev.n

    V(lambda E: E.tensor_scalar(t1b[:, :], S_("k11re")[:, :], 1.0, None, ALU.add),
      waits=[(s_ev, k10_done)], inc=s_v)
    V(lambda E: E.tensor_mul(t2b[:, :], t1b[:, :], t1b[:, :]), inc=s_v)
    V(lambda E: E.tensor_mul(t3b[:, :], S_("k11im")[:, :], S_("k11im")[:, :]), inc=s_v)
    V(lambda E: E.tensor_add(t2b[:, :], t2b[:, :], t3b[:, :]), inc=s_v)
    V(lambda E: E.reciprocal(t2b[:, :], t2b[:, :]), inc=s_v)
    V(lambda E: E.tensor_mul(w2re[:, :], S_("k10re")[:, :], t1b[:, :]), inc=s_v)
    V(lambda E: E.tensor_mul(t3b[:, :], S_("k10im")[:, :], S_("k11im")[:, :]), inc=s_v)
    V(lambda E: E.tensor_add(w2re[:, :], w2re[:, :], t3b[:, :]), inc=s_v)
    V(lambda E: E.tensor_mul(w2re[:, :], w2re[:, :], t2b[:, :]), inc=s_v)
    V(lambda E: E.tensor_mul(w2im[:, :], S_("k10im")[:, :], t1b[:, :]), inc=s_v)
    V(lambda E: E.tensor_mul(t3b[:, :], S_("k10re")[:, :], S_("k11im")[:, :]), inc=s_v)
    V(lambda E: E.tensor_sub(w2im[:, :], w2im[:, :], t3b[:, :]), inc=s_v)
    wv = V(lambda E: E.tensor_mul(w2im[:, :], w2im[:, :], t2b[:, :]), inc=s_v)
    w2b = wv

    V(lambda E: E.tensor_scalar(t1b[:, :], w2re[:, :], cols[:, 4:5], None, ALU.mult),
      waits=[(s_v, w2b)], inc=s_v)
    V(lambda E: E.tensor_scalar(t2b[:, :], w2im[:, :], cols[:, 5:6], None, ALU.mult),
      inc=s_v)
    V(lambda E: E.tensor_sub(t1b[:, :], t1b[:, :], t2b[:, :]), inc=s_v)
    V(lambda E: E.tensor_scalar(t1b[:, :], t1b[:, :], -1.0, cols[:, 2:3],
                                ALU.mult, ALU.add), inc=s_v)
    V(lambda E: E.tensor_scalar(t2b[:, :], w2re[:, :], cols[:, 5:6], None, ALU.mult),
      inc=s_v)
    V(lambda E: E.tensor_scalar(t3b[:, :], w2im[:, :], cols[:, 4:5], None, ALU.mult),
      inc=s_v)
    V(lambda E: E.tensor_add(t2b[:, :], t2b[:, :], t3b[:, :]), inc=s_v)
    V(lambda E: E.tensor_scalar(t2b[:, :], t2b[:, :], -1.0, cols[:, 3:4],
                                ALU.mult, ALU.add), inc=s_v)
    V(lambda E: E.tensor_mul(t3b[:, :], cd_re[:, :], t1b[:, :]), inc=s_v)
    V(lambda E: E.tensor_mul(t4b[:, :], cd_im[:, :], t2b[:, :]), inc=s_v)
    V(lambda E: E.tensor_sub(t3b[:, :], t3b[:, :], t4b[:, :]), inc=s_v)
    V(lambda E: E.tensor_mul(t4b[:, :], cd_re[:, :], t2b[:, :]), inc=s_v)
    V(lambda E: E.tensor_mul(t1b[:, :], cd_im[:, :], t1b[:, :]), inc=s_v)
    V(lambda E: E.tensor_add(t4b[:, :], t4b[:, :], t1b[:, :]), inc=s_v)
    V(lambda E: E.tensor_mul(g2re[:, :], cc["f2re"][:, :], t3b[:, :]), inc=s_v)
    V(lambda E: E.tensor_mul(t1b[:, :], cc["f2im"][:, :], t4b[:, :]), inc=s_v)
    V(lambda E: E.tensor_sub(g2re[:, :], g2re[:, :], t1b[:, :]), inc=s_v)
    V(lambda E: E.tensor_mul(g2im[:, :], cc["f2re"][:, :], t4b[:, :]), inc=s_v)
    V(lambda E: E.tensor_mul(t1b[:, :], cc["f2im"][:, :], t3b[:, :]), inc=s_v)
    g2v = V(lambda E: E.tensor_add(g2im[:, :], g2im[:, :], t1b[:, :]), inc=s_v)
    ctv = V(lambda E: E.tensor_scalar(ctimn[:, :], ctim[:, :], -1.0, None, ALU.mult),
            waits=[(s_ld, lds)], inc=s_v)

    em.sbuf_reset(mS1)
    kstage = {i: al(f"cs_kst{i}", [128, 2048], BF16) for i in range(4)}
    kx1T = {p: al(f"cs_kxT{p}", [128, 4096], BF16) for p in ("re", "im")}
    mS2 = em.sbuf_mark()
    at_re = al("cs_atre", [128, 4224], F)
    at_im = al("cs_atim", [128, 4224], F)
    atT = {p: al(f"cs_atT{p}", [128, 4096], BF16) for p in ("re", "im")}
    gr = {p: al(f"cs_gr{p}", [128, 2048], F) for p in ("re", "im")}

    plans = {"re": ((ctre, g2re), (ctimn, g2im)), "im": ((ctre, g2im), (ctim, g2re))}
    at_t = {"re": at_re, "im": at_im}
    for pl in ("re", "im"):
        for chk in range(8):
            lh = chk // 4
            mm = None
            accs = [(lw, rw, lh) for (lw, rw) in plans[pl]]
            na = len(accs)
            for ai, (lw, rw, lh) in enumerate(accs):
                mm = em.op("tensor", lambda E, lw=lw, rw=rw, lh=lh, chk=chk,
                           ai=ai, na=len(accs): E.matmul(
                               pb[5][:, 0:512],
                               lw[64 * lh:64 * lh + 64, :],
                               rw[64 * lh:64 * lh + 64,
                                  (chk % 4) * 512:(chk % 4) * 512 + 512],
                               start=(ai == 0), stop=(ai == na - 1)),
                           waits=([(s_v, max(g2v, ctv)), (s_ev, s_ev.n)]
                                  if ai == 0 else []),
                           inc=(s_mm if ai == na - 1 else None))
            SC(lambda E, pl=pl, chk=chk: E.activation(
                at_t[pl][:, (chk % 4) * 512 + (chk // 4) * 2048:
                         (chk % 4) * 512 + (chk // 4) * 2048 + 512],
                pb[5][:, 0:512], AF.Identity), waits=[(s_mm, mm)], inc=s_ev)
    at_done = s_ev.n
    dbg_tgt = 0
    if dbg is not None:
        dram, s_st = dbg
        for pi, tt in ((0, at_re), (1, at_im)):
            dbg_tgt = em.op("sync", lambda E, pi=pi, tt=tt: E.dma_start(
                out=bass.AP(dram, pi * 128 * 4224, [[4224, 128], [1, 4096]]),
                in_=tt[:, 0:4096]), waits=[(s_ev, at_done)], inc=s_st, inck=16)

    V(lambda E: E.tensor_copy(at_re[:, 4096:4097], at_re[:, 0:1]),
      waits=[(s_ev, at_done)], inc=s_v)
    V(lambda E: E.tensor_copy(at_im[:, 4096:4097], at_im[:, 0:1]), inc=s_v)
    V(lambda E: E.tensor_copy(atT["re"][:, :], at_re[:, 0:4096]), inc=s_v)
    atbv = V(lambda E: E.tensor_copy(atT["im"][:, :], at_im[:, 0:4096]), inc=s_v)

    gridap = lambda t, off: bass.AP(t, off, [[4224, 128], [1, 32], [32, 64]])
    revap = lambda t: bass.AP(t, 4096, [[4224, 128], [-1, 32], [-32, 64]])
    V(lambda E: E.tensor_add(gr["re"][:, :], gridap(at_re, 0), revap(at_re)),
      waits=[(s_v, atbv)], inc=s_v)
    V(lambda E: E.tensor_scalar(gr["re"][:, :], gr["re"][:, :], 1.0 / 8192.0, None,
                                ALU.mult), inc=s_v)
    V(lambda E: E.tensor_scalar(gr["re"][:, 0:1], gr["re"][:, 0:1], 0.5, None,
                                ALU.mult), inc=s_v)
    V(lambda E: E.tensor_sub(gr["im"][:, :], gridap(at_im, 0), revap(at_im)),
      inc=s_v)
    V(lambda E: E.tensor_scalar(gr["im"][:, :], gr["im"][:, :], 1.0 / 8192.0, None,
                                ALU.mult), inc=s_v)
    grv = V(lambda E: E.tensor_scalar(gr["im"][:, 0:1], at_re[:, 2048:2049],
                                      1.0 / 8192.0, None, ALU.mult), inc=s_v)

    ke_done = 0
    for pi, pl in enumerate(("re", "im")):
        for kb in range(4):
            bank = pb[kb % 2]
            mm = None
            for ksub in range(4):
                kh = kb * 4 + ksub
                mm = em.op("tensor", lambda E, pl=pl, kh=kh, ksub=ksub, bank=bank:
                           E.transpose(
                               bank[:, ksub * 128:(ksub + 1) * 128],
                               bass.AP(gr[pl], kh * 128, [[2048, 128], [1, 128]]),
                               cc["identF"][:, :]),
                           waits=([(s_v, grv), (s_ev, s_ev.n)]
                                  if (ksub == 0) else []),
                           inc=(s_mm if ksub == 3 else None))
            ke_done = SC(lambda E, pi=pi, kb=kb, bank=bank: E.activation(
                kstage[pi][:, kb * 512:(kb + 1) * 512], bank[:, 0:512],
                AF.Identity), waits=[(s_mm, mm)], inc=s_ev)

    for pl in ("re", "im"):
        srcb = atT[pl]
        dstb = kx1T[pl]
        for kb in range(4):
            bank = pt[kb % 2]
            mm = None
            for ksub in range(8):
                k1L = kb * 8 + ksub
                mm = em.op("tensor", lambda E, srcb=srcb, k1L=k1L, ksub=ksub,
                           bank=bank: E.transpose(
                               bank[:, ksub * 128:(ksub + 1) * 128],
                               bass.AP(srcb, k1L, [[4096, 128], [32, 128]]),
                               sb["ident"][:, :]),
                           waits=([(s_v, atbv), (s_ev, s_ev.n)]
                                  if ksub == 0 else []),
                           inc=(s_mm if ksub == 7 else None))
            SC(lambda E, dstb=dstb, kb=kb, bank=bank: E.activation(
                dstb[:, kb * 1024:(kb + 1) * 1024], bank[:, :], AF.Identity),
               waits=[(s_mm, mm)], inc=s_ev)
    atT_done = s_ev.n

    em.sbuf_reset(mS2)
    yk = al("cs_yk", [128, 4096], BF16)
    ykT = {p: al(f"cs_ykT{p}", [128, 4096], BF16) for p in ("re", "im")}
    kfwd = al("cs_kfwd", [128, 4096], BF16)
    kx1 = al("cs_kx1", [128, 4096], BF16)

    if dbg is not None and dbg_tgt:
        em.op("scalar", lambda E, s_st=dbg[1], v=dbg_tgt: E.wait_ge(s_st.h, v),
              inc=s_ev)
    for opi, opl in enumerate(("re", "im")):
        dst = yk if opi == 0 else kfwd
        for k1L in range(32):
            bank = pb[2 + (k1L % 2)]
            mm = None
            for acc in range(2):
                stc = cc["KP1c"] if (opi == 0) == (acc == 0) else cc["KP1s"]
                sgn_handled = True
                rhs = kx1T["re"] if acc == 0 else kx1T["im"]
                st_ap = None
                if opi == 0 and acc == 0:
                    stn = "KP1c"
                elif opi == 0 and acc == 1:
                    stn = "KP1sn"
                elif opi == 1 and acc == 0:
                    stn = "KP1s"
                else:
                    stn = "KP1c"
                mm = em.op("tensor", lambda E, stn=stn, k1L=k1L, rhs=rhs,
                           bank=bank, acc=acc: E.matmul(
                               bank[:, 0:128],
                               cc[stn][:, k1L * 128:(k1L + 1) * 128],
                               rhs[:, k1L * 128:(k1L + 1) * 128],
                               start=(acc == 0), stop=(acc == 1)),
                           waits=([(s_ev, atT_done), (s_ev, s_ev.n)]
                                  if acc == 0 else []),
                           inc=(s_mm if acc == 1 else None))
            SC(lambda E, dst=dst, k1L=k1L, bank=bank: E.activation(
                dst[:, k1L * 128:(k1L + 1) * 128], bank[:, 0:128], AF.Identity),
               waits=[(s_mm, mm)], inc=s_ev)
    yk_done = s_ev.n

    for pl, srcb in (("re", yk), ("im", kfwd)):
        dstb = ykT[pl]
        for cb in range(4):
            bank = pt[cb % 2]
            mm = None
            for cc8 in range(8):
                c = cb * 8 + cc8
                mm = em.op("tensor", lambda E, srcb=srcb, c=c, cc8=cc8, bank=bank:
                           E.transpose(
                               bank[:, cc8 * 128:(cc8 + 1) * 128],
                               bass.AP(srcb, c, [[4096, 128], [32, 128]]),
                               sb["ident"][:, :]),
                           waits=([(s_ev, yk_done), (s_ev, s_ev.n)]
                                  if cc8 == 0 else []),
                           inc=(s_mm if cc8 == 7 else None))
            SC(lambda E, dstb=dstb, cb=cb, bank=bank: E.activation(
                dstb[:, cb * 1024:(cb + 1) * 1024], bank[:, :], AF.Identity),
               waits=[(s_mm, mm)], inc=s_ev)
    ykT_done = s_ev.n

    for chk in range(8):
        bank = pb[4]
        mm = None
        for ai, (stn, rpl) in enumerate((("KP2a", "re"), ("KP2b", "im"))):
            mm = em.op("tensor", lambda E, stn=stn, rpl=rpl, chk=chk, bank=bank,
                       ai=ai: E.matmul(
                           bank[:, 0:512], cc[stn][:, :],
                           ykT[rpl][:, chk * 512:(chk + 1) * 512],
                           start=(ai == 0), stop=(ai == 1)),
                       waits=([(s_ev, ykT_done), (s_ev, s_ev.n)]
                              if ai == 0 else []),
                       inc=(s_mm if ai == 1 else None))
        SC(lambda E, chk=chk, bank=bank: E.activation(
            kfwd[:, chk * 512:(chk + 1) * 512], bank[:, 0:512], AF.Identity),
           waits=[(s_mm, mm)], inc=s_ev)
    kfwd_done = s_ev.n

    for pi, pl in enumerate(("re", "im")):
        W = sb[("W1", "O", pl)]
        for ch in range(8):
            bank = pb[pi % 2]
            mm = em.op("tensor", lambda E, W=W, ch=ch, bank=bank: E.matmul(
                bank[:, 0:512], W[:, :], kfwd[:, ch * 512:(ch + 1) * 512],
                start=True, stop=True),
                waits=[(s_ev, kfwd_done if (pi == 0 and ch == 0) else s_ev.n)],
                inc=s_mm)
            SC(lambda E, ch=ch, bank=bank: E.activation(
                kx1[:, ch * 512:(ch + 1) * 512], bank[:, 0:512], AF.Identity),
               waits=[(s_mm, mm)], inc=s_ev)
        dstb = kx1T[pl]
        for cb in range(4):
            bank = pt[cb % 2]
            mm = None
            for cc8 in range(8):
                c = cb * 8 + cc8
                mm = em.op("tensor", lambda E, c=c, cc8=cc8, bank=bank: E.transpose(
                    bank[:, cc8 * 128:(cc8 + 1) * 128],
                    kx1[:, c * 128:(c + 1) * 128], sb["ident"][:, :]),
                    waits=([(s_ev, s_ev.n)] if cc8 == 0 else []),
                    inc=(s_mm if cc8 == 7 else None))
            SC(lambda E, dstb=dstb, cb=cb, bank=bank: E.activation(
                dstb[:, cb * 1024:(cb + 1) * 1024], bank[:, :], AF.Identity),
               waits=[(s_mm, mm)], inc=s_ev)
    kO1_done = s_ev.n

    em.op("vector", lambda E: E.memset(cols[:, 13:14], 2.0 / 8192.0), inc=s_v)
    sc_t = s_v.n
    for kh in range(16):
        bA = pb[2 + (kh % 2) * 2]
        bB = pb[3 + (kh % 2) * 2]
        mm = None
        for member in range(2):
            k1 = 2 * kh + member
            ro = 64 * member
            rhs_re = bass.AP(kx1T["re"], k1, [[4096, 128], [32, 128]])
            rhs_im = bass.AP(kx1T["im"], k1, [[4096, 128], [32, 128]])
            for kind, (bank, rr) in enumerate(
                    ((bA, rhs_re), (bA, rhs_im), (bB, rhs_re), (bB, rhs_im))):
                st = sb[("P2", "O")][:, (k1 * 4 + kind) * 64:(k1 * 4 + kind + 1) * 64]
                mm = em.op("tensor", lambda E, st=st, rr=rr, bank=bank, ro=ro,
                           kind=kind: E.matmul(
                               bank[ro:ro + 64, 0:128], st, rr,
                               start=(kind % 2 == 0), stop=(kind % 2 == 1),
                               tile_position=(0, ro)),
                           waits=([(s_ev, kO1_done), (s_v, sc_t), (s_ev, s_ev.n)]
                                  if (member == 0 and kind == 0) else []),
                           inc=(s_mm if (member == 1 and kind == 3) else None))
        for bank, pi in ((bA, 2), (bB, 3)):
            SC(lambda E, bank=bank, pi=pi, kh=kh: E.activation(
                bass.AP(kstage[pi], kh * 128, [[2048, 128], [1, 32], [32, 4]]),
                bank[0:128, 0:128], AF.Identity, scale=cols[:, 13:14]),
               waits=[(s_mm, mm)], inc=s_ev)
    ko_done = s_ev.n
    return {"kstage": kstage, "ke_done": ke_done, "ko_done": ko_done,
            "sev": s_ev, "at_re": at_re, "at_im": at_im, "at_done": at_done,
            "gr": gr, "grv": grv, "s_v": s_v, "kfwd": kfwd, "g2re": g2re, "g2im": g2im, "cols": cols,
            "kfwd_done": kfwd_done}


def load_constr_consts(em, nc, con, alloc, s_ld):
    cc = {}

    def put(name, arr, dt):
        npdt = np.float32 if dt is F32 else BF
        arrc = np.ascontiguousarray(np.asarray(arr, dtype=npdt))
        d = nc.inline_tensor(arrc, name + "_cc")
        h = alloc("cc_" + name, list(arrc.shape), dt)
        em.op("sync", lambda E, d=d, h=h: E.dma_start(out=h[:, :], in_=d.ap()),
              inc=s_ld, inck=16)
        return h

    for n in ("G0re", "G0im", "f2re", "f2im", "selA0", "selA1", "selA", "identF"):
        cc[n] = put(n, con[n], F32)
    for n in ("KP1c", "KP1s", "KP1sn", "KP2a", "KP2b"):
        cc[n] = put(n, con[n], BF16)
    return cc


def build_phase1(em, nc, sb, r_d, kf_d, D_d, ufm_d, utm_d, h1tm_d, g_d,
                 xmid_d, xn3_d, alloc, pb, pt, sems, consts_tgt):
    s_mm, s_v, s_ev, s_ld = sems["mm"], sems["v"], sems["ev"], sems["ld"]
    s_rt3 = sems["rt"]
    s_uw2 = sems["uw"]
    s_ur2 = sems["ur"]
    s_k = sems["k"]
    s_hw = sems["hw"]
    s_hr2 = sems["hr"]
    s_gw2 = sems["gw"]
    s_gr2 = sems["gr"]
    s_xmw2 = sems["xmw"]
    s_xnw2 = sems["xnw"]

    dcol = alloc("p1_dcol", [128, 1], F32)
    onecol = alloc("p1_one", [1, 1], F32)
    m0 = em.sbuf_mark()
    rt = [alloc(f"p1_rt{i}", [128, 1024], F32) for i in range(3)]
    ut = [alloc(f"p1_ut{i}", [128, 1024], BF16) for i in range(2)]
    ustage = [alloc(f"p1_ustg{i}", [128, 1024], BF16) for i in range(2)]
    mcol = alloc("p1_mcol", [128, 32], F32)
    sqcol = alloc("p1_sqcol", [128, 32], F32)
    rstd = alloc("p1_rstd", [128, 32], F32)
    negm = alloc("p1_negm", [128, 32], F32)
    sqscr = alloc("p1_sqscr", [128, 1024], F32)

    s_dld = sems["dld"]
    em.op("sync", lambda E: E.dma_start(
        out=onecol[0:1, 0:1], in_=bass.AP(D_d, 0, [[1, 1], [1, 1]])),
        inc=s_dld, inck=16)
    dld = s_dld.n
    mmb = em.op("tensor", lambda E: E.matmul(
        pb[0][:, 0:1], sb["selA"][0:1, :], onecol[0:1, 0:1], start=True, stop=True),
        waits=[(s_dld, dld), (s_ld, consts_tgt)], inc=s_mm)
    dbc = em.op("scalar", lambda E: E.activation(
        dcol[:, 0:1], pb[0][:, 0:1], AF.Identity), waits=[(s_mm, mmb)], inc=s_ev)

    r_guard = [((0, 0)) for _ in range(3)]
    for T in range(32):
        i = T % 3
        ld = em.op("sync", lambda E, T=T, i=i: E.dma_start(
            out=rt[i][:, :],
            in_=bass.AP(r_d, T * 128 * H, [[H, 128], [1, H]])),
            waits=[(s_v, r_guard[i][0]), (s_ev, r_guard[i][1])],
            inc=s_rt3[i], inck=16)
        rs = em.op("vector", lambda E, i=i, T=T: E.reduce_sum(
            mcol[:, T:T + 1], rt[i][:, :], axis=mybir.AxisListType.X),
            waits=[(s_rt3[i], ld)], inc=s_v)
        sq = em.op("scalar", lambda E, i=i, T=T: E.activation(
            sqscr[:, :], rt[i][:, :], AF.Square, accum_out=sqcol[:, T:T + 1]),
            waits=[(s_rt3[i], ld), (s_v, rs)], inc=s_ev)
        r_guard[i] = (rs, sq)
        last_sq = sq
    em.op("vector", lambda E: E.tensor_scalar(
        mcol[:, :], mcol[:, :], 1.0 / H, None, ALU.mult), inc=s_v)
    em.op("vector", lambda E: E.tensor_mul(negm[:, :], mcol[:, :], mcol[:, :]),
          inc=s_v)
    vv = em.op("vector", lambda E: E.scalar_tensor_tensor(
        rstd[:, :], sqcol[:, :], 1.0 / H, negm[:, :], ALU.mult, ALU.subtract),
        waits=[(s_ev, last_sq)], inc=s_v)
    lg = em.op("scalar", lambda E: E.activation(
        rstd[:, :], rstd[:, :], AF.Ln, bias=sb["eps_c"][:, 0:1]),
        waits=[(s_v, vv)], inc=s_ev)
    ex = em.op("scalar", lambda E: E.activation(
        rstd[:, :], rstd[:, :], AF.Exp, scale=sb["neghalf_c"][:, 0:1]), inc=s_ev)
    nm = em.op("vector", lambda E: E.scalar_tensor_tensor(
        negm[:, :], mcol[:, :], -1.0, rstd[:, :], ALU.mult, ALU.mult),
        waits=[(s_ev, ex)], inc=s_v)

    u_guard = [0, 0]
    us_guard = [0, 0]
    for T in range(32):
        i = T % 3
        par = T % 2
        ld = em.op("sync", lambda E, T=T, i=i: E.dma_start(
            out=rt[i][:, :], in_=bass.AP(r_d, T * 128 * H, [[H, 128], [1, H]])),
            waits=[(s_v, r_guard[i][0]), (s_ev, r_guard[i][1])],
            inc=s_rt3[i], inck=16)
        un = em.op("scalar", lambda E, i=i, T=T, par=par: E.activation(
            ut[par][:, :], rt[i][:, :], AF.Identity,
            bias=negm[:, T:T + 1], scale=rstd[:, T:T + 1]),
            waits=[(s_rt3[i], ld), (s_v, nm), (s_uw2[par], u_guard[par])],
            inc=s_ev)
        r_guard[i] = (0, un)
        ug = em.op("sync", lambda E, T=T, par=par: E.dma_start(
            out=bass.AP(utm_d, T * 128 * H, [[H, 128], [1, H]]),
            in_=ut[par][:, :]), waits=[(s_ev, un)], inc=s_uw2[par], inck=16)
        bankT = pt[par]
        mm = None
        for cc in range(8):
            mm = em.op("tensor", lambda E, cc=cc, par=par, bankT=bankT: E.transpose(
                bankT[:, cc * 128:(cc + 1) * 128],
                ut[par][:, cc * 128:(cc + 1) * 128], sb["ident"][:, :]),
                waits=([(s_ev, un), (s_v, us_guard[par])] if cc == 0 else []),
                inc=(s_mm if cc == 7 else None), mmode="t")
        ev = em.op("vector", lambda E, par=par, bankT=bankT: E.tensor_copy(
            ustage[par][:, :], bankT[:, :]), waits=[(s_mm, mm)], inc=s_v)
        ug2 = em.op("sync", lambda E, T=T, par=par: E.dma_start(
            out=bass.AP(ufm_d, T * 128, [[4096, 128], [4096 * 128, 8], [1, 128]]),
            in_=bass.AP(ustage[par], 0, [[1024, 128], [128, 8], [1, 128]])),
            waits=[(s_v, ev)], inc=s_uw2[par], inck=16)
        u_guard[par] = ug2
        us_guard[par] = ev
    uprep_done = [s_uw2[0].n, s_uw2[1].n]

    em.sbuf_reset(m0)
    b = build_conv_bufs(nc, alloc)
    _h1s = alloc("p1_h1stg0", [128, 4096], BF16)
    h1stage = [_h1s, _h1s]
    guards = {"bg": [dbc] + [0] * 5, "pt": [0, 0], "y": [0, 0], "x1": 0,
              "x1t": 0, "s_free": 0, "p_free": 0, "yt": 0, "h1g": None, "upar": 0}
    u_ready = [0, 0]
    ugd = [0, 0]
    hst_guard = [0, 0]

    def load_u2(g, par, waits):
        tg = None
        for qq in range(4):
            inap = bass.AP(ufm_d, (128 * g + 32 * qq) * 4096,
                           [[128, 32], [4096, 32], [1, 128]])
            outap = bass.AP(b["u"][par], qq * 32 * 4096,
                            [[4096, 32], [128, 32], [1, 128]])
            tg = em.op("sync", lambda E, inap=inap, outap=outap: E.dma_start(
                out=outap, in_=inap), waits=waits, inc=s_ur2[par], inck=16)
        return tg

    def load_k(g, waits):
        tg = None
        for pi in range(4):
            pl = [("E", "re"), ("E", "im"), ("O", "re"), ("O", "im")][pi]
            inap = bass.AP(kf_d, (g * 4 + pi) * 128 * 2048, [[2048, 128], [1, 2048]])
            tg = em.op("gpsimd", lambda E, inap=inap, pl=pl: E.dma_start(
                out=b["K"][pl][:, :], in_=inap), waits=waits, inc=s_k, inck=16)
        return tg

    uwb = [(s_uw2[0], uprep_done[0]), (s_uw2[1], uprep_done[1])]
    u_ready[0] = load_u2(0, 0, uwb + [(s_ld, consts_tgt)])
    u_ready[1] = load_u2(1, 1, uwb)
    kr = load_k(0, uwb)
    for g in range(8):
        par = g % 2
        guards["upar"] = par
        h1_ev = build_conv_group(em, nc, sb, b, pb, pt,
                                 (s_mm, s_v, s_ev, s_ur2[par], s_k),
                                 guards, u_ready[par], kr)
        ugd[par] = s_mm.n
        if g + 2 < 8:
            u_ready[par] = load_u2(g + 2, par, [(s_mm, ugd[par])])
        if g + 1 < 8:
            kr = load_k(g + 1, [(s_v, guards["s_free"])])
        hpar = 0
        stg = h1stage[hpar]
        for cb in range(4):
            ti = cb % 2
            mm = None
            for cc in range(8):
                c = cb * 8 + cc
                mm = em.op("tensor", lambda E, c=c, cc=cc, bank=pt[ti]: E.transpose(
                    bank[:, cc * 128:(cc + 1) * 128],
                    b["h1g"][:, c * 128:(c + 1) * 128], sb["ident"][:, :]),
                    waits=([(s_ev, h1_ev), (s_v, guards["pt"][ti]),
                            (s_hw, hst_guard[hpar])] if cc == 0 else []),
                    inc=(s_mm if cc == 7 else None), mmode="t")
            ev = em.op("vector", lambda E, cb=cb, bank=pt[ti], stg=stg: E.tensor_copy(
                bass.AP(stg, 0, [[4096, 128], [1, 32 * 128]])
                if False else
                bass.AP(stg, cb * 8, [[4096, 128], [1, 8], [32, 4], [128, 32]]),
                bank[:, :]), waits=[(s_mm, mm)], inc=s_v)
            guards["pt"][ti] = ev
        hw = em.op("sync", lambda E, g=g, stg=stg: E.dma_start(
            out=bass.AP(h1tm_d, g * 128, [[1024, 128], [128 * 1024, 32], [1, 128]]),
            in_=bass.AP(stg, 0, [[4096, 128], [128, 32], [1, 128]])),
            waits=[(s_v, s_v.n)], inc=s_hw, inck=16)
        hst_guard[hpar] = hw
        guards["h1g"] = (s_mm, s_mm.n)
    conv_done_hw = s_hw.n

    em.sbuf_reset(m0)
    NS = 8
    ht = [alloc(f"p1_ht{i}", [128, 1024], BF16) for i in range(NS)]
    ut2 = [alloc(f"p1_ut2{i}", [128, 1024], BF16) for i in range(NS)]
    rt2 = [alloc(f"p1_rt2{i}", [128, 1024], F32) for i in range(NS)]
    gt = [alloc(f"p1_gt{i}", [128, 1024], BF16) for i in range(NS)]
    xmt = [alloc(f"p1_xmt{i}", [128, 1024], F32) for i in range(NS)]
    xnt = [alloc(f"p1_xnt{i}", [128, 1024], BF16) for i in range(2)]
    yscr = [alloc(f"p1_ys{i}", [128, 1024], F32) for i in range(2)]
    rn = [alloc(f"p1_rn{i}", [128, 1024], F32) for i in range(2)]
    sqs = [alloc("p1_sqs", [128, 1024], F32)] * 2
    stA = [alloc(f"p1_stA{i}", [128, 16], F32) for i in range(2)]
    stB = [alloc(f"p1_stB{i}", [128, 16], F32) for i in range(2)]
    s_hr8 = [em.sem(f"thr{i}") for i in range(NS)]
    s_xmw8 = [em.sem(f"txm{i}") for i in range(NS)]
    s_xnw4 = [em.sem(f"txn{i}") for i in range(2)]

    T_list = [4 * j + c for c in range(4) for j in range(8)]
    vg_slot = [0] * NS
    ge_gd = [0, 0]
    gt_gd = [0] * NS
    sq3_gd = [0] * NS
    xm_gd = [0] * NS
    xn_gd = [0] * 2
    rn_gd = [0] * 2
    nmA_gd = [0, 0]
    nmB_gd = [0, 0]
    conv_mm, conv_v, conv_ev = s_mm.n, s_v.n, s_ev.n
    chunk_tgts = [None] * 4

    def stageA(b):
        bp = b % 2
        sta = stA[bp]
        for i in range(4):
            T = T_list[4 * b + i]
            s = (4 * b + i) % NS
            first = ([(s_hw, conv_done_hw), (s_mm, conv_mm), (s_ev, conv_ev)]
                     if b < 2 else [])
            em.op("sync", lambda E, T=T, s=s: E.dma_start(
                out=ht[s][:, :],
                in_=bass.AP(h1tm_d, T * 128 * H, [[H, 128], [1, H]])),
                waits=first + [(s_v, max(vg_slot[s], conv_v))],
                inc=s_hr8[s], inck=16)
            em.op("sync", lambda E, T=T, s=s: E.dma_start(
                out=ut2[s][:, :],
                in_=bass.AP(utm_d, T * 128 * H, [[H, 128], [1, H]])),
                inc=s_hr8[s], inck=16)
            rl = em.op("sync", lambda E, T=T, s=s: E.dma_start(
                out=rt2[s][:, :],
                in_=bass.AP(r_d, T * 128 * H, [[H, 128], [1, H]])),
                inc=s_hr8[s], inck=16)
            yp = (4 * b + i) % 2
            y1 = em.op("vector", lambda E, s=s, yp=yp: E.scalar_tensor_tensor(
                yscr[yp][:, :], ut2[s][:, :], dcol[:, 0:1], ht[s][:, :],
                ALU.mult, ALU.add),
                waits=[(s_hr8[s], rl), (s_ev, max(dbc, ge_gd[yp]))], inc=s_v)
            ge = em.op("scalar", lambda E, s=s, yp=yp: E.activation(
                gt[s][:, :], yscr[yp][:, :], AF.Gelu),
                waits=[(s_v, max(y1, gt_gd[s]))], inc=s_ev)
            ge_gd[yp] = ge
            em.op("vector", lambda E, s=s, sta=sta, i=i: E.reduce_sum(
                sta[:, i:i + 1], gt[s][:, :], axis=mybir.AxisListType.X),
                waits=[(s_ev, ge)], inc=s_v)
            sq = em.op("scalar", lambda E, s=s, sta=sta, i=i, sp=(4*b+i) % 2:
                       E.activation(sqs[sp][:, :], gt[s][:, :], AF.Square,
                                    accum_out=sta[:, 4 + i:5 + i]),
                waits=[(s_v, nmA_gd[bp])], inc=s_ev)
        return sq

    def chain2(b, sq_t, st2, nm_gd):
        bp = b % 2
        sta = st2[bp]
        em.op("vector", lambda E, sta=sta: E.tensor_scalar(
            sta[:, 0:4], sta[:, 0:4], 1.0 / H, None, ALU.mult), inc=s_v)
        em.op("vector", lambda E, sta=sta: E.tensor_mul(
            sta[:, 12:16], sta[:, 0:4], sta[:, 0:4]), inc=s_v)
        vv = em.op("vector", lambda E, sta=sta: E.scalar_tensor_tensor(
            sta[:, 8:12], sta[:, 4:8], 1.0 / H, sta[:, 12:16],
            ALU.mult, ALU.subtract), waits=[(s_ev, sq_t)], inc=s_v)
        em.op("scalar", lambda E, sta=sta: E.activation(
            sta[:, 8:12], sta[:, 8:12], AF.Ln, bias=sb["eps_c"][:, 0:1]),
            waits=[(s_v, vv)], inc=s_ev)
        ex = em.op("scalar", lambda E, sta=sta: E.activation(
            sta[:, 8:12], sta[:, 8:12], AF.Exp, scale=sb["neghalf_c"][:, 0:1]),
            inc=s_ev)
        nm = em.op("vector", lambda E, sta=sta: E.scalar_tensor_tensor(
            sta[:, 12:16], sta[:, 0:4], -1.0, sta[:, 8:12],
            ALU.mult, ALU.mult), waits=[(s_ev, ex)], inc=s_v)
        nm_gd[bp] = nm
        return nm

    def stageB(b, nm_t):
        bp = b % 2
        sta = stA[bp]
        stb = stB[bp]
        for i in range(4):
            T = T_list[4 * b + i]
            s = (4 * b + i) % NS
            rb = em.op("vector", lambda E, s=s, i=i, sta=sta: E.tensor_scalar(
                rn[i % 2][:, :], rt2[s][:, :], sta[:, 12 + i:13 + i], None,
                ALU.add), waits=[(s_v, max(nm_t, rn_gd[i % 2]))], inc=s_v)
            stt = em.op("vector", lambda E, s=s, i=i, sta=sta:
                        E.scalar_tensor_tensor(
                            xmt[s][:, :], gt[s][:, :], sta[:, 8 + i:9 + i],
                            rn[i % 2][:, :], ALU.mult, ALU.add),
                waits=[(s_xmw8[s], xm_gd[s]), (s_ev, sq3_gd[s])], inc=s_v)
            gt_gd[s] = stt
            rn_gd[i % 2] = stt
            vg_slot[s] = stt
            xm_gd[s] = em.op("sync", lambda E, T=T, s=s: E.dma_start(
                out=bass.AP(xmid_d, T * 128 * H, [[H, 128], [1, H]]),
                in_=xmt[s][:, :]), waits=[(s_v, stt)], inc=s_xmw8[s], inck=16)
            em.op("vector", lambda E, s=s, i=i, stb=stb: E.reduce_sum(
                stb[:, i:i + 1], xmt[s][:, :], axis=mybir.AxisListType.X),
                inc=s_v)
            sq3 = em.op("scalar", lambda E, s=s, i=i, stb=stb, sp=(4*b+i) % 2:
                        E.activation(sqs[sp][:, :], xmt[s][:, :], AF.Square,
                                     accum_out=stb[:, 4 + i:5 + i]),
                waits=[(s_v, max(s_v.n, nmB_gd[bp]))], inc=s_ev)
            sq3_gd[s] = sq3
        return sq3

    def stageC(b, nm3_t):
        bp = b % 2
        stb = stB[bp]
        for i in range(4):
            T = T_list[4 * b + i]
            s = (4 * b + i) % NS
            ts = em.op("vector", lambda E, s=s, i=i, stb=stb: E.tensor_scalar(
                xnt[i % 2][:, :], xmt[s][:, :], stb[:, 8 + i:9 + i],
                stb[:, 12 + i:13 + i], ALU.mult, ALU.add),
                waits=[(s_v, nm3_t), (s_xnw4[i % 2], xn_gd[i % 2])], inc=s_v)
            xn_gd[i % 2] = em.op("sync", lambda E, T=T, i=i: E.dma_start(
                out=bass.AP(xn3_d, T * 128 * H, [[H, 128], [1, H]]),
                in_=xnt[i % 2][:, :]), waits=[(s_v, ts)],
                inc=s_xnw4[i % 2], inck=16)

    sqA = {}
    nmA = {}
    sqB = {}
    nmB = {}
    for blk in range(10):
        if blk < 8:
            sqA[blk] = stageA(blk)
            nmA[blk] = chain2(blk, sqA[blk], stA, nmA_gd)
        if 1 <= blk <= 8:
            sqB[blk - 1] = stageB(blk - 1, nmA[blk - 1])
            nmB[blk - 1] = chain2(blk - 1, sqB[blk - 1], stB, nmB_gd)
        if blk >= 2:
            stageC(blk - 2, nmB[blk - 2])
            if (blk - 2) % 2 == 1:
                ck = (blk - 2) // 2
                chunk_tgts[ck] = ([(sm, sm.n) for sm in s_xmw8] +
                                  [(sm, sm.n) for sm in s_xnw4])
    return s_xmw8 + s_xnw4, chunk_tgts





from concourse.bass_utils import run_bass_kernel_spmd

CORES = list(range(8))


def _build_constr_nc():
    nc = bass.Bass("TRN2", target_bir_lowering=False, debug=False)
    inp = {}
    for n in ("B_re", "B_im", "P_re", "P_im", "Q_re", "Q_im", "diag_re", "diag_im"):
        inp[n] = nc.dram_tensor(n, [64], F32, kind="ExternalInput")
    inp["step"] = nc.dram_tensor("step", [1], F32, kind="ExternalInput")
    inp["ctre"] = nc.dram_tensor("ctre", [64, 128], F32, kind="ExternalInput")
    inp["ctim"] = nc.dram_tensor("ctim", [64, 128], F32, kind="ExternalInput")
    kout = nc.dram_tensor("kout", [4, 128, 2048], BF16, kind="ExternalOutput")
    em = Em(nc)
    alloc = lambda name, shape, dt: nc.alloc_sbuf_tensor(name, shape, dt)
    s_mm = em.sem("mm"); s_v = em.sem("v"); s_ev = em.sem("ev")
    s_ld = em.sem("ld"); s_st = em.sem("st")
    em.set_auto("tensor", s_mm)
    em.set_auto("vector", s_v)
    em.set_auto("scalar", s_ev)
    con = make_consts()
    concc = make_constr_consts()
    sb = load_consts(em, nc, con, alloc, s_ld, subset="constr")
    cc = load_constr_consts(em, nc, concc, alloc, s_ld)
    pb = [nc.alloc_psum_tensor(f"pb{i}", [128, 512], F32) for i in range(6)]
    pt = [nc.alloc_psum_tensor(f"pt{i}", [128, 1024], BF16) for i in range(2)]
    r = build_construction(em, nc, sb, cc, inp, alloc, pb, pt,
                           (s_mm, s_v, s_ev, s_ld), None)
    for pi in range(4):
        em.op("sync", lambda E, pi=pi, r=r: E.dma_start(
            out=bass.AP(kout, pi * 128 * 2048, [[2048, 128], [1, 2048]]),
            in_=r["kstage"][pi][:, :]), waits=[(s_ev, r["ko_done"])],
            inc=s_st, inck=16)
    def fin(E, t=s_st):
        E.wait_ge(t.h, t.n)
    em.q["sync"].append(fin)
    em.finalize()
    return nc


def _build_phase1_nc():
    nc = bass.Bass("TRN2", target_bir_lowering=False, debug=False)
    r_d = nc.dram_tensor("r", [4096, 1024], F32, kind="ExternalInput")
    kf_d = nc.dram_tensor("kf", [8, 4, 128, 2048], BF16, kind="ExternalInput")
    D_d = nc.dram_tensor("D", [1], F32, kind="ExternalInput")
    ufm_d = nc.dram_tensor("ufm", [1024, 4096], BF16, kind="Internal")
    utm_d = nc.dram_tensor("utm", [4096, 1024], BF16, kind="Internal")
    h1tm_d = nc.dram_tensor("h1tm", [4096, 1024], BF16, kind="Internal")
    g_d = nc.dram_tensor("gsc", [4096, 1024], BF16, kind="Internal")
    xmid_d = nc.dram_tensor("xmid", [4096, 1024], F32, kind="ExternalOutput")
    xn3_d = nc.dram_tensor("xn3", [4096, 1024], BF16, kind="ExternalOutput")
    em = Em(nc)
    alloc = lambda name, shape, dt: nc.alloc_sbuf_tensor(name, shape, dt)
    sems = {n: em.sem(n) for n in ("mm", "v", "ev", "ld", "k", "hw", "dld")}
    for n in ("uw", "ur", "hr", "gr", "xmw", "xnw", "gw"):
        sems[n] = [em.sem(n + "0"), em.sem(n + "1")]
    sems["rt"] = [em.sem(f"rt{i}") for i in range(3)]
    em.set_auto("tensor", sems["mm"])
    em.set_auto("vector", sems["v"])
    em.set_auto("scalar", sems["ev"])
    con = make_consts()
    sb = load_consts(em, nc, con, alloc, sems["ld"])
    pb = [nc.alloc_psum_tensor(f"pb{i}", [128, 512], F32) for i in range(6)]
    pt = [nc.alloc_psum_tensor(f"pt{i}", [128, 1024], BF16) for i in range(2)]
    consts_tgt = sems["ld"].n
    st_sems, _ck = build_phase1(em, nc, sb, r_d, kf_d, D_d, ufm_d, utm_d,
                                h1tm_d, g_d, xmid_d, xn3_d, alloc, pb, pt,
                                sems, consts_tgt)
    def fin(E, ss=st_sems, tg=[s.n for s in st_sems]):
        for s, t in zip(ss, tg):
            E.wait_ge(s.h, t)
    em.q["sync"].append(fin)
    em.finalize()
    return nc


def _build_phase2_nc():
    nc = bass.Bass("TRN2", target_bir_lowering=False, debug=False)
    xn3_d = nc.dram_tensor("xn3", [8, 512, 1024], BF16, kind="ExternalInput")
    xmid_d = nc.dram_tensor("xmid", [8, 512, 1024], F32, kind="ExternalInput")
    names = ("wf", "wi", "wc", "uf", "ui", "uc", "wo", "uo")
    w_d = {n: nc.dram_tensor(n, [1024, 1024], F32, kind="ExternalInput")
           for n in names}
    b_d = {n: nc.dram_tensor(n, [1024], F32, kind="ExternalInput")
           for n in ("bf", "bi", "bc", "bo")}
    ic_d = nc.dram_tensor("in_c", [1024], F32, kind="ExternalInput")
    ident_d = nc.dram_tensor("ident", [128, 128], F32, kind="ExternalInput")
    out_d = nc.dram_tensor("out", [8, 512, 1024], F32, kind="ExternalOutput")
    em = Em(nc)
    alloc = lambda name, shape, dt: nc.alloc_sbuf_tensor(name, shape, dt)
    pb = [nc.alloc_psum_tensor(f"pb{i}", [128, 512], F32) for i in range(6)]
    pt = [nc.alloc_psum_tensor(f"pt{i}", [128, 1024], BF16) for i in range(2)]
    s_st2 = build_phase2(em, nc, xn3_d, xmid_d, w_d, b_d, ic_d, out_d, ident_d,
                         alloc, pb, pt)
    def fin(E, s_st2=s_st2):
        for s in s_st2:
            E.wait_ge(s.h, s.n)
    em.q["sync"].append(fin)
    em.finalize()
    return nc


ALL_GROUP = [list(range(8))]
from concourse.aot_env import allow_ici_all_to_all


def _p1_sems(em):
    sems = {n: em.sem(n) for n in ("ld", "k", "hw", "dld")}
    for n in ("uw", "ur", "hr", "gr", "xmw", "xnw", "gw"):
        sems[n] = [em.sem(n + "0"), em.sem(n + "1")]
    sems["rt"] = [em.sem(f"rt{i}") for i in range(3)]
    return sems


def _build_fused_nc():
    nc = bass.Bass("TRN2", target_bir_lowering=False, debug=False)
    inp = {}
    for n in ("B_re", "B_im", "P_re", "P_im", "Q_re", "Q_im", "diag_re", "diag_im"):
        inp[n] = nc.dram_tensor(n, [64], F32, kind="ExternalInput")
    inp["step"] = nc.dram_tensor("step", [1], F32, kind="ExternalInput")
    inp["ctre"] = nc.dram_tensor("ctre", [64, 128], F32, kind="ExternalInput")
    inp["ctim"] = nc.dram_tensor("ctim", [64, 128], F32, kind="ExternalInput")
    r_d = nc.dram_tensor("r", [4096, 1024], F32, kind="ExternalInput")
    D_d = nc.dram_tensor("D", [1], F32, kind="ExternalInput")
    wnames = ("wf", "wi", "wc", "uf", "ui", "uc", "wo", "uo")
    w_d = {n: nc.dram_tensor(n, [1024, 1024], F32, kind="ExternalInput")
           for n in wnames}
    b_d = {n: nc.dram_tensor(n, [1024], F32, kind="ExternalInput")
           for n in ("bf", "bi", "bc", "bo")}
    ic_d = nc.dram_tensor("in_c", [1024], F32, kind="ExternalInput")
    ident_d = nc.dram_tensor("ident", [128, 128], F32, kind="ExternalInput")
    out_d = nc.dram_tensor("out", [8, 512, 1024], F32, kind="ExternalOutput")
    kst_d = nc.dram_tensor("kst", [4, 128, 2048], BF16, kind="Internal")
    kfull_d = nc.dram_tensor("kfull", [8, 4, 128, 2048], BF16, kind="Internal")
    ufm_d = nc.dram_tensor("ufm", [1024, 4096], BF16, kind="Internal")
    utm_d = nc.dram_tensor("utm", [4096, 1024], BF16, kind="Internal")
    h1tm_d = nc.dram_tensor("h1tm", [4096, 1024], BF16, kind="Internal")
    g_d = nc.dram_tensor("gsc", [4096, 1024], BF16, kind="Internal")
    xmid_d = nc.dram_tensor("xmid", [4096, 1024], F32, kind="Internal")
    xn3_d = nc.dram_tensor("xn3", [4096, 1024], BF16, kind="Internal")
    xmid_sh = nc.dram_tensor("xmid_sh", [8, 512, 1024], F32, kind="Internal")
    xn3_sh = nc.dram_tensor("xn3_sh", [8, 512, 1024], BF16, kind="Internal")

    em = Em(nc)
    alloc = lambda name, shape, dt: nc.alloc_sbuf_tensor(name, shape, dt)
    pb = [nc.alloc_psum_tensor(f"pb{i}", [128, 512], F32) for i in range(6)]
    pt = [nc.alloc_psum_tensor(f"pt{i}", [128, 1024], BF16) for i in range(2)]
    s_cc = em.sem("cc")
    s_kst = em.sem("kstst")
    s_ldc = em.sem("ldc")

    mA = em.sbuf_mark()
    s_mm = em.sem("mm")
    s_v = em.sem("v")
    s_ev = em.sem("ev")
    em.set_auto("tensor", s_mm)
    em.set_auto("vector", s_v)
    em.set_auto("scalar", s_ev)
    con = make_consts()
    concc = make_constr_consts()
    em.cur_scope = "constr"
    sbc = load_consts(em, nc, con, alloc, s_ldc, subset="constr", pref="c_")
    ccb = load_constr_consts(em, nc, concc, alloc, s_ldc)
    rcon = build_construction(em, nc, sbc, ccb, inp, alloc, pb, pt,
                              (s_mm, s_v, s_ev, s_ldc), None)
    for pi in range(4):
        em.op("sync", lambda E, pi=pi, rcon=rcon: E.dma_start(
            out=bass.AP(kst_d, pi * 128 * 2048, [[2048, 128], [1, 2048]]),
            in_=rcon["kstage"][pi][:, :]), waits=[(s_ev, rcon["ko_done"])],
            inc=s_kst, inck=16)
    kst_done = s_kst.n

    NK = 4 * 128 * 2048

    def ccag(E):
        E.wait_ge(s_kst.h, kst_done)
        E.collective_compute(
            "AllGather", mybir.AluOpType.bypass, replica_groups=ALL_GROUP,
            ins=[bass.AP(kst_d, 0, [[1, NK]])],
            outs=[bass.AP(kfull_d, 0, [[1, 8 * NK]])],
        ).then_inc(s_cc.h, 1)
    em.q["gpsimd"].append(ccag)
    s_cc.n += 1

    mm_c, v_c, ev_c = s_mm.n, s_v.n, s_ev.n
    em.sbuf_reset(mA)
    gateA = [(s_mm, mm_c), (s_v, v_c), (s_ev, ev_c), (s_kst, kst_done)]
    for eng in ("tensor", "vector", "scalar", "sync"):
        em.fence(eng, gateA)
    em.fence("gpsimd", gateA + [(s_cc, 1)])

    em.cur_scope = "p1"
    sems = _p1_sems(em)
    sems["mm"], sems["v"], sems["ev"] = s_mm, s_v, s_ev
    sb = load_consts(em, nc, con, alloc, sems["ld"])
    consts_tgt = sems["ld"].n
    st_sems, chunk_tgts = build_phase1(em, nc, sb, r_d, kfull_d, D_d, ufm_d,
                                       utm_d, h1tm_d, g_d, xmid_d, xn3_d,
                                       alloc, pb, pt, sems, consts_tgt)
    st_tgts = [s.n for s in st_sems]

    N2 = 4096 * 1024

    def cca2(E):
        for s, t in zip(st_sems, st_tgts):
            E.wait_ge(s.h, t)
        E.collective_compute(
            "AllToAll", mybir.AluOpType.bypass, replica_groups=ALL_GROUP,
            ins=[bass.AP(xn3_d, 0, [[1, N2]])],
            outs=[bass.AP(xn3_sh, 0, [[1, N2]])],
        ).then_inc(s_cc.h, 1)
        E.collective_compute(
            "AllToAll", mybir.AluOpType.bypass, replica_groups=ALL_GROUP,
            ins=[bass.AP(xmid_d, 0, [[1, N2]])],
            outs=[bass.AP(xmid_sh, 0, [[1, N2]])],
        ).then_inc(s_cc.h, 1)
    em.q["gpsimd"].append(cca2)
    s_cc.n += 2

    mm1, v1, ev1 = s_mm.n, s_v.n, s_ev.n
    em.sbuf_reset(mA)
    gateB = [(s_mm, mm1), (s_v, v1), (s_ev, ev1)]
    for eng in ("tensor", "vector", "scalar", "gpsimd"):
        em.fence(eng, gateB)
    em.fence("sync", gateB + [(s_cc, 3)])

    em.cur_scope = "p2"
    s_st2 = build_phase2(em, nc, xn3_sh, xmid_sh, w_d, b_d, ic_d, out_d,
                         ident_d, alloc, pb, pt)

    def fin(E, ss=s_st2, tg=[s.n for s in s_st2]):
        for s, t in zip(ss, tg):
            E.wait_ge(s.h, t)
    em.q["sync"].append(fin)
    em.finalize()
    return nc


_NC_CACHE = {}


def _fused_nc():
    if "nc" not in _NC_CACHE:
        _NC_CACHE["nc"] = _build_fused_nc()
    return _NC_CACHE["nc"]


def kernel(**inputs):
    f32 = np.float32
    C_re = np.asarray(inputs["L0_attn_C_re"], f32)
    C_im = np.asarray(inputs["L0_attn_C_im"], f32)
    rfull = np.asarray(inputs["L0_attn_r"], f32)
    D = np.ascontiguousarray(np.asarray(inputs["L0_attn_D"], f32))
    scal = {n: np.ascontiguousarray(np.asarray(inputs[f"L0_attn_{n}"], f32))
            for n in ("B_re", "B_im", "P_re", "P_im", "Q_re", "Q_im",
                      "diag_re", "diag_im", "step")}
    wmats = {n: np.ascontiguousarray(np.asarray(inputs[f"L0_ffn_{n}"], f32).T)
             for n in ("wf", "wi", "wc", "uf", "ui", "uc", "wo", "uo")}
    bvecs = {n: np.ascontiguousarray(np.asarray(inputs[f"L0_ffn_{n}"], f32))
             for n in ("bf", "bi", "bc", "bo")}
    in_c = np.ascontiguousarray(np.asarray(inputs["L0_ffn_in_c"], f32))
    ident = np.eye(128, dtype=f32)

    in_maps = []
    for core in range(8):
        m = dict(scal)
        m["ctre"] = np.ascontiguousarray(C_re[128 * core:128 * core + 128, :].T)
        m["ctim"] = np.ascontiguousarray(C_im[128 * core:128 * core + 128, :].T)
        m["r"] = np.ascontiguousarray(rfull[core])
        m["D"] = D
        m.update(wmats)
        m.update(bvecs)
        m["in_c"] = in_c
        m["ident"] = ident
        in_maps.append(m)
    with allow_ici_all_to_all():
        nc = _fused_nc()
        res = run_bass_kernel_spmd(nc, in_maps, core_ids=CORES)
    out = np.concatenate([res.results[i]["out"] for i in range(8)], axis=1)
    return out
